# version 02
# baseline (speedup 1.0000x reference)
import numpy as np
from contextlib import ExitStack

import concourse.bass as bass
import concourse.mybir as mybir
from concourse.bass_utils import run_bass_kernel_spmd

F32 = mybir.dt.float32
BF16 = mybir.dt.bfloat16
AF = mybir.ActivationFunctionType
ALU = mybir.AluOpType
AX = mybir.AxisListType

H = 8
NQH = 16
C = 2048
T = 2048
TO = 1024
NE = 17
NCH = 16
GATE_OFFSET = 6.906768
LOG_ATT_SCALE = float(np.log(1.0 / np.sqrt(128.0)))
LOG_EPS = float(np.log(1e-6))

OWN_BLOCKS = {0: [0, 1, 2, 3, 12, 13, 14, 15], 1: [4, 5, 6, 7, 8, 9, 10, 11]}
TILE_T0 = {0: [0, 1536], 1: [512, 1024]}
TILE_NCH = [8, 16]
TILE_MASKED = {0: list(range(0, 8)), 1: list(range(8, 16))}


class Sem:
    def __init__(self, ctx, nc, name):
        self.h = ctx.enter_context(nc.semaphore(name))
        self.n = 0

    def tick(self, k=1):
        self.n += k
        return (self.h, k)

    @property
    def at(self):
        return (self.h, self.n)


_DVE_DRAIN_OPS = {
    "tensor_tensor", "tensor_scalar", "tensor_scalar_add", "tensor_scalar_mul",
    "tensor_scalar_sub", "tensor_scalar_max", "tensor_scalar_min",
    "tensor_single_scalar", "tensor_copy", "tensor_add", "tensor_sub",
    "tensor_mul", "tensor_max", "reduce_sum", "reduce_max", "tensor_reduce",
    "reciprocal", "scalar_tensor_tensor", "tensor_tensor_scan", "select",
    "transpose", "memset", "iota", "tensor_relu",
}


class _VProxy:

    def __init__(self, eng):
        object.__setattr__(self, "_e", eng)

    def __getattr__(self, name):
        attr = getattr(object.__getattribute__(self, "_e"), name)
        if name in _DVE_DRAIN_OPS:
            eng = object.__getattribute__(self, "_e")

            def wrapped(*a, **k):
                inst = attr(*a, **k)
                eng.drain()
                return inst
            return wrapped
        return attr


class Phase:
    def __init__(self, nc):
        self.nc = nc
        self.q = {"tensor": [], "vector": [], "scalar": [], "gpsimd": [], "sync": []}

    def emit(self, eng, name, *args, waits=(), tick=None, ticks=(), **kwargs):
        waits = tuple((h, c) for (h, c) in waits if c > 0)
        allticks = tuple(t for t in ((tick,) if tick else ()) + tuple(ticks) if t)

        def fn(e, name=name, args=args, kwargs=kwargs, waits=waits,
               allticks=allticks):
            for h, c in waits:
                e.wait_ge(h, c)
            inst = getattr(e, name)(*args, **kwargs)
            for h, k in allticks:
                inst.then_inc(h, k)
        self.q[eng].append(fn)

    def raw(self, eng, fn):
        self.q[eng].append(fn)

    def flush(self):
        with self.nc.Block() as blk:
            for name, deco in (("sync", blk.sync), ("gpsimd", blk.gpsimd),
                               ("tensor", blk.tensor), ("vector", blk.vector),
                               ("scalar", blk.scalar)):
                ops = self.q[name]
                if ops:
                    if name == "vector":
                        deco(lambda eng, ops=ops: [fn(_VProxy(eng)) for fn in ops])
                    else:
                        deco(lambda eng, ops=ops: [fn(eng) for fn in ops])
        self.q = {k: [] for k in self.q}


def build_program(debug=False):
    nc = bass.Bass("TRN2", target_bir_lowering=False)

    def din(name, shape):
        return nc.dram_tensor(name, list(shape), F32, kind="ExternalInput").ap()

    xT_full = din("xT_full", (C, T))
    xT_own = din("xT_own", (C, TO))
    csT_d = din("csT", (128, 16))
    thr_d = din("thr", (128, 32))
    mrep_d = din("maskrep", (2, 8, T))
    wq_p = din("wq_p", (16, 128, 16, 128))
    wk_p = din("wk_p", (8, 128, 16, 128))
    wv_n = din("wv_n", (C, 1024))
    wc_p = din("wc_p", (16, 128, 16, 128))
    wgat_p = din("wgat_p", (128, 16, 8))
    wmod_n = din("wmod_n", (C, 12288))
    wr_p = din("wr_p", (128, 16, 16))
    fbias_d = din("fbias", (1, 16))
    bc_d = din("bc_p", (128, 16))
    wg_p = din("wg_p", (NE, 8, 128, 16, 128))
    wu_p = din("wu_p", (NE, 8, 128, 16, 128))
    wd_p = din("wd_p", (NE, 16, 128, 8, 128))

    yT_d = nc.dram_tensor("yT", [C, TO], F32, kind="ExternalOutput").ap()
    scr8 = nc.dram_tensor("scr8", [8, TO], F32, kind="Internal").ap()
    wscr = nc.dram_tensor("wscr", [16, TO], F32, kind="Internal").ap()

    dbg = {}
    if debug:
        for nm, shape in [("d_mod", (128, 96)), ("d_logG", (8, T)),
                          ("d_mrow", (1, 12288)), ("d_cs", (128, 16)),
                          ("d_scl", (128, 16, 8)), ("d_pref", (8, 2)),
                          ("d_k", (128, 8, T)), ("d_v", (128, 16, 1024)),
                          ("d_q", (128, 16, 512)), ("d_xmid", (128, NCH, TO)),
                          ("d_w", (16, 8, 128)), ("d_h2", (128, NCH, TO))]:
            dbg[nm] = nc.dram_tensor(nm, list(shape), F32, kind="ExternalOutput").ap()

    ctx = ExitStack()
    with ctx:
        en = ctx.enter_context

        XT = en(nc.sbuf_tensor("XT", [128, NCH, TO], F32, side="left"))
        iota = en(nc.sbuf_tensor("iota", [128, 512], F32, side="left"))
        ONESf = en(nc.sbuf_tensor("ONESf", [128, 128], F32, side="left"))
        ONESb = en(nc.sbuf_tensor("ONESb", [128, 128], BF16, side="left"))
        idf = en(nc.sbuf_tensor("idf", [128, 128], F32, side="left"))
        modcol = en(nc.sbuf_tensor("modcol", [128, 96], F32, side="left"))
        sc32 = en(nc.sbuf_tensor("sc32", [128, 32], F32, side="left"))
        gabc = en(nc.sbuf_tensor("gabc", [128, 16], F32, side="left"))
        sclCol = en(nc.sbuf_tensor("sclCol", [128, 16, 8], F32, side="left"))
        thr = en(nc.sbuf_tensor("thr_sb", [128, 32], F32, side="left"))
        bc_sb = en(nc.sbuf_tensor("bc_sb", [128, 16], F32, side="left"))
        csT_sb = en(nc.sbuf_tensor("csT_sb", [128, 16], F32, side="left"))
        csT_bf = en(nc.sbuf_tensor("csT_bf", [128, 16], BF16, side="left"))
        csT_f = en(nc.sbuf_tensor("csT_f", [128, 16], F32, side="left"))
        fb_sb = en(nc.sbuf_tensor("fb_sb", [1, 16], F32, side="left"))
        fb_bf = en(nc.sbuf_tensor("fb_bf", [1, 16], BF16, side="left"))
        wr_sb = en(nc.sbuf_tensor("wr_sb", [128, 16, 16], BF16, side="left"))
        wgat_sb = en(nc.sbuf_tensor("wgat_sb", [128, 16, 8], BF16, side="left"))
        prefx = en(nc.sbuf_tensor("prefx", [8, 2], F32, side="left"))
        cst8 = en(nc.sbuf_tensor("cst8", [8, 4], F32, side="left"))

        PS = en(nc.psum_tensor("PS", [128, 4096], F32))

        def bank(k):
            return PS[:, 512 * k:512 * (k + 1)]

        def brow(k, rows=1, width=512, off=0):
            return PS[0:rows, 512 * k + off:512 * k + off + width]

        sD = Sem(ctx, nc, "sD")
        sG = Sem(ctx, nc, "sG")
        sT = Sem(ctx, nc, "sT")
        sV = Sem(ctx, nc, "sV")
        sA = Sem(ctx, nc, "sA")
        sO = Sem(ctx, nc, "sO")

        ph = Phase(nc)

        with nc.sbuf_tensor("wslf", [128, 4, 512], F32, side="right") as wslf, \
             nc.sbuf_tensor("mrow", [1, 12288], F32, side="right") as mrow:

            ph.emit("gpsimd", "iota", iota[:, :], [[-1, 512]], channel_multiplier=1,
                    allow_small_or_imprecise_dtypes=True)
            ph.emit("gpsimd", "memset", ONESf[:, :], 1.0)
            ph.emit("gpsimd", "memset", ONESb[:, :], 1.0)
            ph.emit("gpsimd", "memset", cst8[:, 0:1], -GATE_OFFSET)
            ph.emit("gpsimd", "memset", cst8[:, 1:2], LOG_ATT_SCALE)
            ph.emit("gpsimd", "memset", cst8[:, 2:3], LOG_EPS, tick=sG.tick())
            w_init = sG.n
            ph.emit("gpsimd", "dma_start", out=wr_sb[:, :, :], in_=wr_p[:, :, :],
                    tick=sG.tick(16))
            ph.emit("gpsimd", "dma_start", out=wgat_sb[:, :, :], in_=wgat_p[:, :, :],
                    tick=sG.tick(16))

            ph.emit("sync", "dma_start", out=csT_sb[:, :], in_=csT_d[:, :],
                    tick=sD.tick(16))
            ph.emit("sync", "dma_start", out=thr[:, :], in_=thr_d[:, :],
                    tick=sD.tick(16))
            ph.emit("sync", "dma_start", out=bc_sb[:, :], in_=bc_d[:, :],
                    tick=sD.tick(16))
            ph.emit("sync", "dma_start", out=fb_sb[:, :], in_=fbias_d[:, :],
                    tick=sD.tick(16))
            w_small = sD.n

            ph.emit("vector", "tensor_single_scalar", idf[:, :], iota[:, 0:128],
                    0.0, ALU.is_equal, waits=[(sG.h, w_init)], tick=sV.tick())

            ph.emit("scalar", "activation", csT_f[:, :], csT_sb[:, :], AF.Silu,
                    waits=[(sD.h, w_small)], tick=sA.tick())
            ph.emit("scalar", "activation", fb_bf[:, :], fb_sb[:, :], AF.Copy,
                    tick=sA.tick())
            w_silu = sA.n

            fr_buf = {}
            fr_ps = {}
            for t in range(24):
                for s2 in range(NCH):
                    b = (t * NCH + s2) % 4
                    ph.emit("sync", "dma_start",
                            out=wslf[:, b, :],
                            in_=wmod_n[128 * s2:128 * (s2 + 1),
                                       512 * t:512 * (t + 1)],
                            waits=[(sT.h, fr_buf.get(b, 0))], tick=sD.tick(16))
                    w_dma = sD.n
                    waits = [(sD.h, w_dma), (sA.h, w_silu)]
                    if s2 == 0:
                        waits.append((sV.h, fr_ps.get(t % 2, 0)))
                    ph.emit("tensor", "matmul", brow(t % 2), csT_f[:, s2:s2 + 1],
                            wslf[:, b, :], start=(s2 == 0), stop=(s2 == 15),
                            waits=waits, tick=sT.tick())
                    fr_buf[b] = sT.n
                ph.emit("vector", "tensor_copy", mrow[0:1, 512 * t:512 * (t + 1)],
                        brow(t % 2), waits=[(sT.h, sT.n)], tick=sV.tick())
                fr_ps[t % 2] = sV.n

            if debug:
                ph.emit("sync", "dma_start", out=dbg["d_mrow"][:, :],
                        in_=mrow[0:1, :], waits=[(sV.h, sV.n)], tick=sO.tick(16))
                ph.emit("sync", "dma_start", out=dbg["d_cs"][:, :],
                        in_=csT_f[:, :], waits=[(sA.h, sA.n)], tick=sO.tick(16))

            def t_cols(te, w=sV.n):
                te.wait_ge(sV.h, w)
                for jj in range(96):
                    inst = te.matmul(PS[:, 1024 + jj:1024 + jj + 1],
                                     mrow[0:1, 128 * jj:128 * (jj + 1)],
                                     ONESf[0:1, 0:1],
                                     start=(jj == 0), stop=(jj == 95))
                inst.then_inc(sT.h, 1)
            ph.raw("tensor", t_cols)
            sT.n += 1

            ph.emit("scalar", "activation", modcol[:, :], PS[:, 1024:1120], AF.Copy,
                    waits=[(sT.h, sT.n)], tick=sA.tick())

            def v_clamp(v, w=sA.n):
                v.wait_ge(sA.h, w)
                for base in (32, 80):
                    sl = modcol[:, base:base + 16]
                    v.tensor_single_scalar(sc32[:, 0:16], sl, 0.0, ALU.is_ge)
                    v.tensor_scalar(sc32[:, 0:16], sc32[:, 0:16], 2.0, -1.0,
                                    ALU.mult, ALU.add)
                    v.tensor_scalar_mul(sc32[:, 16:32], sl, -1.0)
                    v.tensor_tensor(sc32[:, 16:32], sl, sc32[:, 16:32], ALU.max)
                    v.tensor_scalar_max(sc32[:, 16:32], sc32[:, 16:32], 0.25)
                    v.tensor_tensor(sl, sc32[:, 0:16], sc32[:, 16:32], ALU.mult)
                v.tensor_scalar_add(modcol[:, 16:32], modcol[:, 16:32], 1.0)
                v.tensor_scalar_add(modcol[:, 64:80], modcol[:, 64:80], 1.0)
                v.tensor_tensor(gabc[:, :], modcol[:, 32:48], bc_sb[:, :],
                                ALU.mult).then_inc(sV.h, 1)
            ph.raw("vector", v_clamp)
            sV.n += 1

            if debug:
                ph.emit("sync", "dma_start", out=dbg["d_mod"][:, :],
                        in_=modcol[:, :], waits=[(sV.h, sV.n)], tick=sO.tick(16))
            ph.flush()

        with nc.sbuf_tensor("KB", [128, H, T], BF16, side="right") as KB, \
             nc.sbuf_tensor("VB", [128, 16, 1024], BF16, side="right") as VB, \
             nc.sbuf_tensor("gG", [8, T], F32, side="right") as gG:

            with ExitStack() as p1ctx:
                e1 = p1ctx.enter_context
                HTt = e1(nc.sbuf_tensor("HTt", [128, NCH, 512], BF16, side="right"))
                xs1 = e1(nc.sbuf_tensor("xs1", [128, 2, 512], F32, side="right"))
                xb1 = e1(nc.sbuf_tensor("xb1", [128, 2, 512], BF16, side="right"))
                sq1 = e1(nc.sbuf_tensor("sq1", [128, 2, 512], BF16, side="right"))
                tf1 = e1(nc.sbuf_tensor("tf1", [128, 2, 512], F32, side="right"))
                rowM = e1(nc.sbuf_tensor("rowM", [1, 512], F32, side="right"))
                rowA = e1(nc.sbuf_tensor("rowA", [1, 512], F32, side="right"))
                rowB = e1(nc.sbuf_tensor("rowB", [1, 512], F32, side="right"))
                Abc = e1(nc.sbuf_tensor("Abc", [128, 512], F32, side="right"))
                Bbc = e1(nc.sbuf_tensor("Bbc", [128, 512], F32, side="right"))
                wk2 = e1(nc.sbuf_tensor("wk2", [128, 2, 16, 128], BF16, side="right"))
                wv2 = e1(nc.sbuf_tensor("wv2", [128, 2, 1024], BF16, side="right"))

                fr_xs = {}
                fr_xb = {}
                fr_tf = {}
                fr_wk = {}
                fr_wv = {}
                fr_kbank = {}
                w_stats_free = 0
                for tg in range(4):
                    gsl = slice(512 * tg, 512 * (tg + 1))
                    for chh in range(NCH):
                        b = chh % 2
                        ph.emit("sync", "dma_start", out=xs1[:, b, :],
                                in_=xT_full[128 * chh:128 * (chh + 1), gsl],
                                waits=[(sA.h, fr_xs.get(b, 0))], tick=sD.tick(16))
                        ph.emit("scalar", "activation", xb1[:, b, :], xs1[:, b, :],
                                AF.Copy,
                                waits=[(sD.h, sD.n)] +
                                      ([fr_xb[b]] if b in fr_xb else []),
                                tick=sA.tick())
                        ph.emit("scalar", "activation", sq1[:, b, :], xs1[:, b, :],
                                AF.Square, tick=sA.tick())
                        fr_xs[b] = sA.n
                        ph.emit("tensor", "matmul", brow(4, rows=8), ONESb[:, 0:8],
                                xb1[:, b, :], start=(chh == 0), stop=(chh == 15),
                                waits=[(sA.h, sA.n)] +
                                      ([(sA.h, w_stats_free)] if chh == 0 else []),
                                tick=sT.tick())
                        ph.emit("tensor", "matmul", brow(5, rows=8), ONESb[:, 0:8],
                                sq1[:, b, :], start=(chh == 0), stop=(chh == 15),
                                tick=sT.tick())
                        fr_xb[b] = (sT.h, sT.n)

                    def v_rows1(v, w=sT.n):
                        v.wait_ge(sT.h, w)
                        v.tensor_scalar_mul(rowM[:, :], brow(4), 1.0 / 2048.0)
                        v.tensor_scalar_mul(rowA[:, :], brow(5), 1.0 / 2048.0)
                        v.tensor_tensor(rowB[:, :], rowM[:, :], rowM[:, :], ALU.mult)
                        v.tensor_tensor(rowA[:, :], rowA[:, :], rowB[:, :],
                                        ALU.subtract)
                        v.tensor_scalar_add(rowA[:, :], rowA[:, :],
                                            1e-5).then_inc(sV.h, 1)
                    ph.raw("vector", v_rows1)
                    sV.n += 1
                    ph.emit("scalar", "activation", rowB[:, :], rowA[:, :], AF.Sqrt,
                            waits=[(sV.h, sV.n)], tick=sA.tick())

                    def v_rows2(v, w=sA.n):
                        v.wait_ge(sA.h, w)
                        v.reciprocal(rowA[:, :], rowB[:, :])
                        v.tensor_tensor(rowB[:, :], rowM[:, :], rowA[:, :], ALU.mult)
                        v.tensor_scalar_mul(rowB[:, :], rowB[:, :],
                                            -1.0).then_inc(sV.h, 1)
                    ph.raw("vector", v_rows2)
                    sV.n += 1
                    ph.emit("tensor", "matmul", brow(6, rows=128), ONESf[0:1, :],
                            rowA[0:1, :], start=True, stop=True,
                            waits=[(sV.h, sV.n)], tick=sT.tick())
                    ph.emit("tensor", "matmul", brow(7, rows=128), ONESf[0:1, :],
                            rowB[0:1, :], start=True, stop=True, tick=sT.tick())
                    ph.emit("scalar", "activation", Abc[:, :], bank(6), AF.Copy,
                            waits=[(sT.h, sT.n)], tick=sA.tick())
                    ph.emit("scalar", "activation", Bbc[:, :], bank(7), AF.Copy,
                            tick=sA.tick())
                    w_abc = sA.n

                    for chh in range(NCH):
                        b = chh % 2
                        ph.emit("sync", "dma_start", out=xs1[:, b, :],
                                in_=xT_full[128 * chh:128 * (chh + 1), gsl],
                                waits=[(sA.h, fr_xs.get(b, 0))], tick=sD.tick(16))
                        ph.emit("scalar", "activation", xb1[:, b, :], xs1[:, b, :],
                                AF.Copy,
                                waits=[(sD.h, sD.n)] +
                                      ([fr_xb[b]] if b in fr_xb else []),
                                tick=sA.tick())
                        fr_xs[b] = sA.n

                        def v_h(v, b=b, w=max(sA.n, w_abc), wtf=fr_tf.get(b, 0)):
                            v.wait_ge(sA.h, max(w, wtf))
                            v.tensor_tensor(tf1[:, b, :], xb1[:, b, :], Abc[:, :],
                                            ALU.mult)
                            v.tensor_tensor(tf1[:, b, :], tf1[:, b, :], Bbc[:, :],
                                            ALU.add).then_inc(sV.h, 1)
                        ph.raw("vector", v_h)
                        sV.n += 1
                        fr_xb[b] = (sV.h, sV.n)
                        ph.emit("scalar", "activation", HTt[:, chh, :], tf1[:, b, :],
                                AF.Identity,
                                scale=modcol[:, 16 + chh:17 + chh],
                                bias=modcol[:, chh:chh + 1],
                                waits=[(sV.h, sV.n)], tick=sA.tick())
                        fr_tf[b] = sA.n
                    w_ht = sA.n

                    for m in range(H):
                        b = m % 2
                        ph.emit("gpsimd", "dma_start", out=wk2[:, b, :, :],
                                in_=wk_p[m, :, :, :],
                                waits=[(sT.h, fr_wk.get(b, 0))], tick=sG.tick(16))

                        def t_k(te, b=b, wg=sG.n,
                                wa=max(w_ht, fr_kbank.get(b, 0))):
                            te.wait_ge(sG.h, wg)
                            te.wait_ge(sA.h, wa)
                            for chh in range(NCH):
                                inst = te.matmul(bank(b), wk2[:, b, chh, :],
                                                 HTt[:, chh, :],
                                                 start=(chh == 0), stop=(chh == 15))
                            inst.then_inc(sT.h, 1)
                        ph.raw("tensor", t_k)
                        sT.n += 1
                        fr_wk[b] = sT.n
                        ph.emit("scalar", "activation", KB[:, m, gsl], bank(b),
                                AF.Copy, waits=[(sT.h, sT.n)], tick=sA.tick())
                        fr_kbank[b] = sA.n

                    def t_g(te, w=sA.n):
                        te.wait_ge(sA.h, w)
                        for chh in range(NCH):
                            inst = te.matmul(brow(2, rows=8), wgat_sb[:, chh, :],
                                             HTt[:, chh, :],
                                             start=(chh == 0), stop=(chh == 15))
                        inst.then_inc(sT.h, 1)
                    ph.raw("tensor", t_g)
                    sT.n += 1
                    ph.emit("scalar", "activation", gG[0:8, gsl], brow(2, rows=8),
                            AF.Copy, waits=[(sT.h, sT.n)], tick=sA.tick())
                    w_g = sA.n

                    for chh in range(NCH):
                        b = chh % 2
                        ph.emit("gpsimd", "dma_start", out=wv2[:, b, :],
                                in_=wv_n[128 * chh:128 * (chh + 1), :],
                                waits=[(sT.h, fr_wv.get(b, 0))], tick=sG.tick(16))

                        def t_v(te, chh=chh, b=b, wg=sG.n, wa=w_g):
                            te.wait_ge(sG.h, wg)
                            if chh == 0:
                                te.wait_ge(sA.h, wa)
                            for tc in range(4):
                                for nt in range(2):
                                    inst = te.matmul(
                                        bank(tc * 2 + nt),
                                        HTt[:, chh, 128 * tc:128 * (tc + 1)],
                                        wv2[:, b, 512 * nt:512 * (nt + 1)],
                                        start=(chh == 0), stop=(chh == 15))
                            inst.then_inc(sT.h, 1)
                        ph.raw("tensor", t_v)
                        sT.n += 1
                        fr_wv[b] = sT.n
                    w_vmm = sT.n

                    for tc in range(4):
                        for nt in range(2):
                            ph.emit("scalar", "activation",
                                    VB[:, 4 * tg + tc, 512 * nt:512 * (nt + 1)],
                                    bank(tc * 2 + nt), AF.Copy,
                                    waits=[(sT.h, w_vmm)], tick=sA.tick())
                    w_stats_free = sA.n

                if debug:
                    ph.emit("gpsimd", "dma_start", out=dbg["d_k"][:, :, :],
                            in_=KB[:, :, :], waits=[(sA.h, sA.n)], tick=sO.tick(16))
                    ph.emit("gpsimd", "dma_start", out=dbg["d_v"][:, :, :],
                            in_=VB[:, :, :], tick=sO.tick(16))
                ph.flush()

            with nc.sbuf_tensor("gX", [8, T], F32, side="right") as gX, \
                 nc.sbuf_tensor("gHh", [8, T], F32, side="right") as gHh, \
                 nc.sbuf_tensor("gCum", [8, T], F32, side="right") as gCum, \
                 nc.sbuf_tensor("gZ", [8, T], F32, side="right") as gZ, \
                 nc.sbuf_tensor("mr_sb", [8, T], F32, side="right") as mr_sb:

                ph.emit("gpsimd", "memset", gZ[:, :], 0.0, tick=sG.tick())
                w_z = sG.n
                ph.emit("scalar", "activation", gX[:, :], gG[:, :], AF.Exp,
                        bias=cst8[:, 0:1], scale=-1.0, tick=sA.tick())

                def v_series(v, w=sA.n, wz=w_z):
                    v.wait_ge(sA.h, w)
                    v.wait_ge(sG.h, wz)
                    v.tensor_scalar(gHh[:, :], gX[:, :], -1.0 / 3.0, 0.5,
                                    ALU.mult, ALU.add)
                    v.tensor_tensor(gHh[:, :], gHh[:, :], gX[:, :], ALU.mult)
                    v.tensor_scalar_add(gHh[:, :], gHh[:, :], -1.0)
                    v.tensor_tensor(gHh[:, :], gHh[:, :], gX[:, :], ALU.mult)
                    v.tensor_tensor_scan(gCum[:, :], gHh[:, :], gZ[:, :],
                                         0.0, ALU.add, ALU.add).then_inc(sV.h, 1)
                ph.raw("vector", v_series)
                sV.n += 1

                for r in range(2):
                    ph.emit("sync", "dma_start", out=mr_sb[:, :],
                            in_=mrep_d[r, :, :],
                            waits=[(sV.h, sV.n if r else 0)], tick=sD.tick(16))

                    def v_pref(v, r=r, w=sD.n):
                        v.wait_ge(sD.h, w)
                        v.tensor_tensor(gX[:, :], gHh[:, :], mr_sb[:, :], ALU.mult)
                        v.reduce_sum(prefx[:, r:r + 1], gX[:, :],
                                     AX.X).then_inc(sV.h, 1)
                    ph.raw("vector", v_pref)
                    sV.n += 1

                ph.emit("scalar", "activation", gHh[:, :], gCum[:, :], AF.Exp,
                        bias=cst8[:, 1:2], scale=-0.5,
                        waits=[(sV.h, sV.n)], tick=sA.tick())

                def t_tr(te, w=sA.n):
                    te.wait_ge(sA.h, w)
                    for cc in range(16):
                        inst = te.transpose(PS[:, 8 * cc:8 * (cc + 1)],
                                            gHh[0:8, 128 * cc:128 * (cc + 1)],
                                            idf[0:8, 0:8])
                    inst.then_inc(sT.h, 1)
                ph.raw("tensor", t_tr)
                sT.n += 1
                ph.emit("scalar", "activation", sclCol[:, :, :], PS[:, 0:128],
                        AF.Copy, waits=[(sT.h, sT.n)], tick=sA.tick())

                if debug:
                    ph.emit("sync", "dma_start", out=dbg["d_logG"][:, :],
                            in_=gCum[:, :], waits=[(sA.h, sA.n)], tick=sO.tick(16))
                    ph.emit("sync", "dma_start", out=dbg["d_scl"][:, :, :],
                            in_=sclCol[:, :, :], tick=sO.tick(16))
                    ph.emit("sync", "dma_start", out=dbg["d_pref"][:, :],
                            in_=prefx[:, :], tick=sO.tick(16))
                ph.flush()

            with ExitStack() as p3ctx:
                e3 = p3ctx.enter_context
                QTt = e3(nc.sbuf_tensor("QTt", [128, 16, 512], BF16, side="right"))
                HOy = e3(nc.sbuf_tensor("HOy", [128, 16, 512], BF16, side="right"))
                ws2 = e3(nc.sbuf_tensor("ws2", [128, 2, 16, 128], BF16, side="right"))
                xb3 = e3(nc.sbuf_tensor("xb3", [128, 2, 512], BF16, side="right"))
                sq3 = e3(nc.sbuf_tensor("sq3", [128, 2, 512], BF16, side="right"))
                tf3 = e3(nc.sbuf_tensor("tf3", [128, 2, 512], F32, side="right"))
                rowM3 = e3(nc.sbuf_tensor("rowM3", [1, 512], F32, side="right"))
                rowA3 = e3(nc.sbuf_tensor("rowA3", [1, 512], F32, side="right"))
                rowB3 = e3(nc.sbuf_tensor("rowB3", [1, 512], F32, side="right"))
                Abc3 = e3(nc.sbuf_tensor("Abc3", [128, 512], F32, side="right"))
                Bbc3 = e3(nc.sbuf_tensor("Bbc3", [128, 512], F32, side="right"))
                go1 = e3(nc.sbuf_tensor("go1", [8, 512], F32, side="right"))
                go2 = e3(nc.sbuf_tensor("go2", [8, 512], F32, side="right"))
                goz = e3(nc.sbuf_tensor("goz", [8, 512], F32, side="right"))
                Ab2 = e3(nc.sbuf_tensor("Ab2", [128, 2, 512], BF16, side="right"))

                ph.emit("sync", "dma_start", out=XT[:, :, :],
                        in_=xT_own.rearrange("(c p) t -> p c t", p=128),
                        tick=sD.tick(16))
                w_xt = sD.n
                ph.emit("gpsimd", "memset", goz[:, :], 0.0, tick=sG.tick())
                w_z3 = sG.n

                fr_xb3 = {}
                fr_tf3 = {}
                fr_ws = {}
                fr_qbank = {}
                fr_stats3 = 0
                for tau in range(2):
                    tsl = slice(512 * tau, 512 * tau + 512)
                    for chh in range(NCH):
                        b = chh % 2
                        ph.emit("scalar", "activation", xb3[:, b, :],
                                XT[:, chh, tsl], AF.Copy,
                                waits=[(sD.h, w_xt)] +
                                      ([fr_xb3[b]] if b in fr_xb3 else []),
                                tick=sA.tick())
                        ph.emit("scalar", "activation", sq3[:, b, :],
                                XT[:, chh, tsl], AF.Square, tick=sA.tick())
                        ph.emit("tensor", "matmul", brow(4, rows=8), ONESb[:, 0:8],
                                xb3[:, b, :], start=(chh == 0), stop=(chh == 15),
                                waits=[(sA.h, sA.n)] +
                                      ([(sA.h, fr_stats3)] if chh == 0 else []),
                                tick=sT.tick())
                        ph.emit("tensor", "matmul", brow(5, rows=8), ONESb[:, 0:8],
                                sq3[:, b, :], start=(chh == 0), stop=(chh == 15),
                                tick=sT.tick())
                        fr_xb3[b] = (sT.h, sT.n)

                    def v_r31(v, w=sT.n):
                        v.wait_ge(sT.h, w)
                        v.tensor_scalar_mul(rowM3[:, :], brow(4), 1.0 / 2048.0)
                        v.tensor_scalar_mul(rowA3[:, :], brow(5), 1.0 / 2048.0)
                        v.tensor_tensor(rowB3[:, :], rowM3[:, :], rowM3[:, :],
                                        ALU.mult)
                        v.tensor_tensor(rowA3[:, :], rowA3[:, :], rowB3[:, :],
                                        ALU.subtract)
                        v.tensor_scalar_add(rowA3[:, :], rowA3[:, :],
                                            1e-5).then_inc(sV.h, 1)
                    ph.raw("vector", v_r31)
                    sV.n += 1
                    ph.emit("scalar", "activation", rowB3[:, :], rowA3[:, :],
                            AF.Sqrt, waits=[(sV.h, sV.n)], tick=sA.tick())

                    def v_r32(v, w=sA.n):
                        v.wait_ge(sA.h, w)
                        v.reciprocal(rowA3[:, :], rowB3[:, :])
                        v.tensor_tensor(rowB3[:, :], rowM3[:, :], rowA3[:, :],
                                        ALU.mult)
                        v.tensor_scalar_mul(rowB3[:, :], rowB3[:, :],
                                            -1.0).then_inc(sV.h, 1)
                    ph.raw("vector", v_r32)
                    sV.n += 1
                    ph.emit("tensor", "matmul", brow(6, rows=128), ONESf[0:1, :],
                            rowA3[0:1, :], start=True, stop=True,
                            waits=[(sV.h, sV.n)], tick=sT.tick())
                    ph.emit("tensor", "matmul", brow(7, rows=128), ONESf[0:1, :],
                            rowB3[0:1, :], start=True, stop=True, tick=sT.tick())
                    ph.emit("scalar", "activation", Abc3[:, :], bank(6), AF.Copy,
                            waits=[(sT.h, sT.n)], tick=sA.tick())
                    ph.emit("scalar", "activation", Bbc3[:, :], bank(7), AF.Copy,
                            tick=sA.tick())
                    w_abc = sA.n

                    for chh in range(NCH):
                        b = chh % 2
                        ph.emit("scalar", "activation", xb3[:, b, :],
                                XT[:, chh, tsl], AF.Copy,
                                waits=[(sA.h, w_abc)] +
                                      ([fr_xb3[b]] if b in fr_xb3 else []),
                                tick=sA.tick())

                        def v_h3(v, b=b, w=sA.n, wtf=fr_tf3.get(b, 0)):
                            v.wait_ge(sA.h, max(w, wtf))
                            v.tensor_tensor(tf3[:, b, :], xb3[:, b, :], Abc3[:, :],
                                            ALU.mult)
                            v.tensor_tensor(tf3[:, b, :], tf3[:, b, :],
                                            Bbc3[:, :], ALU.add).then_inc(sV.h, 1)
                        ph.raw("vector", v_h3)
                        sV.n += 1
                        fr_xb3[b] = (sV.h, sV.n)
                        ph.emit("scalar", "activation", HOy[:, chh, :], tf3[:, b, :],
                                AF.Identity,
                                scale=modcol[:, 16 + chh:17 + chh],
                                bias=modcol[:, chh:chh + 1],
                                waits=[(sV.h, sV.n)], tick=sA.tick())
                        fr_tf3[b] = sA.n
                    w_ho = sA.n

                    for m in range(16):
                        b = m % 2
                        ph.emit("gpsimd", "dma_start", out=ws2[:, b, :, :],
                                in_=wq_p[m, :, :, :],
                                waits=[(sT.h, fr_ws.get(b, 0))], tick=sG.tick(16))

                        def t_q(te, b=b, wg=sG.n,
                                wa=max(w_ho, fr_qbank.get(b, 0))):
                            te.wait_ge(sG.h, wg)
                            te.wait_ge(sA.h, wa)
                            for chh in range(NCH):
                                inst = te.matmul(bank(b), ws2[:, b, chh, :],
                                                 HOy[:, chh, :],
                                                 start=(chh == 0), stop=(chh == 15))
                            inst.then_inc(sT.h, 1)
                        ph.raw("tensor", t_q)
                        sT.n += 1
                        fr_ws[b] = sT.n
                        ph.emit("scalar", "activation", QTt[:, m, :], bank(b),
                                AF.Copy, waits=[(sT.h, sT.n)], tick=sA.tick())
                        fr_qbank[b] = sA.n
                    w_q = sA.n

                    def t_go(te, w=w_q):
                        te.wait_ge(sA.h, w)
                        for chh in range(NCH):
                            inst = te.matmul(brow(2, rows=8), wgat_sb[:, chh, :],
                                             HOy[:, chh, :],
                                             start=(chh == 0), stop=(chh == 15))
                        inst.then_inc(sT.h, 1)
                    ph.raw("tensor", t_go)
                    sT.n += 1
                    ph.emit("scalar", "activation", go1[:, :], brow(2, rows=8),
                            AF.Exp, bias=cst8[:, 0:1], scale=-1.0,
                            waits=[(sT.h, sT.n)], tick=sA.tick())

                    def v_gos(v, tau=tau, w=sA.n, wz=w_z3):
                        v.wait_ge(sA.h, w)
                        v.wait_ge(sG.h, wz)
                        v.tensor_scalar(go2[:, :], go1[:, :], -1.0 / 3.0, 0.5,
                                        ALU.mult, ALU.add)
                        v.tensor_tensor(go2[:, :], go2[:, :], go1[:, :], ALU.mult)
                        v.tensor_scalar_add(go2[:, :], go2[:, :], -1.0)
                        v.tensor_tensor(go2[:, :], go2[:, :], go1[:, :], ALU.mult)
                        v.tensor_tensor_scan(go1[:, :], go2[:, :], goz[:, :],
                                             0.0, ALU.add, ALU.add)
                        v.tensor_scalar_add(go1[:, :], go1[:, :],
                                            prefx[:, tau:tau + 1]).then_inc(sV.h, 1)
                    ph.raw("vector", v_gos)
                    sV.n += 1
                    ph.emit("scalar", "activation", go2[:, :], go1[:, :], AF.Exp,
                            bias=cst8[:, 2:3], scale=-1.0,
                            waits=[(sV.h, sV.n)], tick=sA.tick())
                    ph.emit("sync", "dma_start", out=scr8[:, tsl], in_=go2[:, :],
                            waits=[(sA.h, sA.n)], tick=sD.tick(16))
                    w_eps_out = sD.n

                    def v_gabc(v, tau=tau, wq=w_q):
                        v.wait_ge(sA.h, wq)
                        for chh in range(NCH):
                            v.tensor_scalar_add(
                                XT[:, chh, 512 * tau:512 * tau + 512],
                                XT[:, chh, 512 * tau:512 * tau + 512],
                                gabc[:, chh:chh + 1])
                        v.nop().then_inc(sV.h, 1)
                    ph.raw("vector", v_gabc)
                    sV.n += 1
                    w_pre = sV.n

                    nI = TILE_NCH[tau]
                    fr_eps = {}
                    fr_pS = {}
                    fr_Ab = {}
                    fr_pY = {}
                    fr_pN = {}
                    fr_rnb = [0]
                    rn_rows = {}
                    pend = [None]

                    def emit_tail(hh):
                        ph.emit("tensor", "matmul", brow(6, rows=128), ONESf[0:1, :],
                                rn_rows[hh][0:1, :], start=True, stop=True,
                                waits=[(sV.h, pend[0][1]), (sV.h, fr_rnb[0])],
                                tick=sT.tick())
                        ph.emit("scalar", "activation", Abc3[:, :], bank(6), AF.Copy,
                                waits=[(sT.h, sT.n)], tick=sA.tick())
                        ph.emit("vector", "tensor_tensor", HOy[:, hh, :],
                                bank(2 + hh % 2), Abc3[:, :], ALU.mult,
                                waits=[(sA.h, sA.n)], tick=sV.tick())
                        fr_pY[hh % 2] = sV.n
                        fr_rnb[0] = sV.n

                    for h in range(NQH):
                        kv = h // 2
                        eb = h % 2
                        ph.emit("sync", "dma_start",
                                out=(rowA3 if eb == 0 else rowB3)[0:1, :],
                                in_=scr8[kv:kv + 1, tsl],
                                waits=[(sD.h, w_eps_out),
                                       (sV.h, fr_eps.get(eb, 0))],
                                tick=sD.tick(16))
                        w_er = sD.n

                        for I in range(nI):
                            pb = I % 2
                            ph.emit("tensor", "matmul", bank(pb),
                                    KB[:, kv, 128 * I:128 * (I + 1)],
                                    QTt[:, h, :], start=True, stop=True,
                                    waits=([(sA.h, w_q)] if I == 0 else []) +
                                          [(sA.h, fr_pS.get(pb, 0))],
                                    tick=sT.tick())
                            ph.emit("scalar", "activation", Ab2[:, pb, :], bank(pb),
                                    AF.Square, scale=sclCol[:, I, kv:kv + 1],
                                    waits=[(sT.h, sT.n),
                                           (sT.h, fr_Ab.get(pb, 0))],
                                    tick=sA.tick())
                            fr_pS[pb] = sA.n
                            if I in TILE_MASKED[tau]:
                                ph.emit("vector", "scalar_tensor_tensor",
                                        Ab2[:, pb, :], iota[:, :],
                                        thr[:, 16 * tau + I:16 * tau + I + 1],
                                        Ab2[:, pb, :], ALU.is_le, ALU.mult,
                                        waits=[(sA.h, sA.n)], tick=sV.tick())
                                w_ready = [(sV.h, sV.n)]
                            else:
                                w_ready = [(sA.h, sA.n)]
                            ph.emit("tensor", "matmul", bank(2 + h % 2),
                                    VB[:, I, 128 * kv:128 * (kv + 1)],
                                    Ab2[:, pb, :],
                                    start=(I == 0), stop=(I == nI - 1),
                                    waits=w_ready +
                                          ([(sV.h, fr_pY.get(h % 2, 0))]
                                           if I == 0 else []),
                                    tick=sT.tick())
                            ph.emit("tensor", "matmul", brow(4 + h % 2, rows=8),
                                    ONESb[:, 0:8], Ab2[:, pb, :],
                                    start=(I == 0), stop=(I == nI - 1),
                                    waits=([(sV.h, fr_pN.get(h % 2, 0))]
                                           if I == 0 else []),
                                    tick=sT.tick())
                            fr_Ab[pb] = sT.n
                        w_avn = sT.n

                        if pend[0] is not None:
                            emit_tail(pend[0][0])

                        rn_rows[h] = go1 if h % 2 == 0 else go2

                        def v_norm(v, h=h, eb=eb, w=w_avn, we=w_er,
                                   dst=rn_rows[h]):
                            v.wait_ge(sT.h, w)
                            v.wait_ge(sD.h, we)
                            v.tensor_tensor(brow(4 + h % 2), brow(4 + h % 2),
                                            (rowA3 if eb == 0 else rowB3)[0:1, :],
                                            ALU.add)
                            v.reciprocal(dst[0:1, :],
                                         brow(4 + h % 2)).then_inc(sV.h, 1)
                        ph.raw("vector", v_norm)
                        sV.n += 1
                        fr_eps[eb] = sV.n
                        fr_pN[h % 2] = sV.n
                        pend[0] = (h, sV.n)
                    emit_tail(pend[0][0])
                    pend[0] = None
                    w_quad = sV.n

                    fr_wcb = {}
                    for m in range(16):
                        b = m % 2
                        ph.emit("gpsimd", "dma_start", out=ws2[:, b, :, :],
                                in_=wc_p[m, :, :, :],
                                waits=[(sT.h, fr_ws.get(b, 0))], tick=sG.tick(16))

                        def t_wc(te, m=m, b=b, wg=sG.n,
                                 wv_=max(w_quad, w_pre) if m == 0 else 0,
                                 wp=fr_wcb.get(m % 2, 0)):
                            te.wait_ge(sG.h, wg)
                            if wv_:
                                te.wait_ge(sV.h, wv_)
                            if wp:
                                te.wait_ge(sV.h, wp)
                            for k in range(16):
                                inst = te.matmul(bank(6 + m % 2), ws2[:, b, k, :],
                                                 HOy[:, k, :],
                                                 start=(k == 0), stop=(k == 15))
                            inst.then_inc(sT.h, 1)
                        ph.raw("tensor", t_wc)
                        sT.n += 1
                        fr_ws[b] = sT.n
                        ph.emit("vector", "scalar_tensor_tensor",
                                XT[:, m, 512 * tau:512 * tau + 512],
                                bank(6 + m % 2), modcol[:, 32 + m:33 + m],
                                XT[:, m, 512 * tau:512 * tau + 512],
                                ALU.mult, ALU.add,
                                waits=[(sT.h, sT.n)], tick=sV.tick())
                        fr_wcb[m % 2] = sV.n
                    fr_stats3 = sA.n

                if debug:
                    ph.emit("sync", "dma_start", out=dbg["d_xmid"][:, :, :],
                            in_=XT[:, :, :], waits=[(sV.h, sV.n)], tick=sO.tick(16))
                    ph.emit("gpsimd", "dma_start", out=dbg["d_q"][:, :, :],
                            in_=QTt[:, :, :], waits=[(sV.h, sV.n)], tick=sO.tick(16))
                ph.flush()

        with nc.sbuf_tensor("h2", [128, NCH, TO], BF16, side="right") as h2:
            with ExitStack() as p4ctx:
                e4 = p4ctx.enter_context
                xb4 = e4(nc.sbuf_tensor("xb4", [128, 2, 512], BF16, side="right"))
                sq4 = e4(nc.sbuf_tensor("sq4", [128, 2, 512], BF16, side="right"))
                tf4 = e4(nc.sbuf_tensor("tf4", [128, 2, 512], F32, side="right"))
                rowM4 = e4(nc.sbuf_tensor("rowM4", [1, 512], F32, side="right"))
                rowA4 = e4(nc.sbuf_tensor("rowA4", [1, 512], F32, side="right"))
                rowB4 = e4(nc.sbuf_tensor("rowB4", [1, 512], F32, side="right"))
                Abc4 = e4(nc.sbuf_tensor("Abc4", [128, 512], F32, side="right"))
                Bbc4 = e4(nc.sbuf_tensor("Bbc4", [128, 512], F32, side="right"))
                ef = e4(nc.sbuf_tensor("ef", [128, 16], F32, side="right"))
                pf = e4(nc.sbuf_tensor("pf", [128, 16], F32, side="right"))
                qf = e4(nc.sbuf_tensor("qf", [128, 16], F32, side="right"))
                colA = e4(nc.sbuf_tensor("colA", [128, 1], F32, side="right"))
                colB = e4(nc.sbuf_tensor("colB", [128, 1], F32, side="right"))
                colC = e4(nc.sbuf_tensor("colC", [128, 1], F32, side="right"))
                wfull = e4(nc.sbuf_tensor("wfull", [128, 16], F32, side="right"))
                wT = e4(nc.sbuf_tensor("wT", [16, 8, 128], F32, side="right"))

                fr_xb4 = {}
                fr_tf4 = {}
                fr_stats4 = 0
                for tau in range(2):
                    tsl = slice(512 * tau, 512 * tau + 512)
                    for chh in range(NCH):
                        b = chh % 2
                        ph.emit("scalar", "activation", xb4[:, b, :],
                                XT[:, chh, tsl], AF.Copy,
                                waits=([fr_xb4[b]] if b in fr_xb4 else []),
                                tick=sA.tick())
                        ph.emit("scalar", "activation", sq4[:, b, :],
                                XT[:, chh, tsl], AF.Square, tick=sA.tick())
                        ph.emit("tensor", "matmul", brow(4, rows=8), ONESb[:, 0:8],
                                xb4[:, b, :], start=(chh == 0), stop=(chh == 15),
                                waits=[(sA.h, sA.n)] +
                                      ([(sA.h, fr_stats4)] if chh == 0 else []),
                                tick=sT.tick())
                        ph.emit("tensor", "matmul", brow(5, rows=8), ONESb[:, 0:8],
                                sq4[:, b, :], start=(chh == 0), stop=(chh == 15),
                                tick=sT.tick())
                        fr_xb4[b] = (sT.h, sT.n)

                    def v_r41(v, w=sT.n):
                        v.wait_ge(sT.h, w)
                        v.tensor_scalar_mul(rowM4[:, :], brow(4), 1.0 / 2048.0)
                        v.tensor_scalar_mul(rowA4[:, :], brow(5), 1.0 / 2048.0)
                        v.tensor_tensor(rowB4[:, :], rowM4[:, :], rowM4[:, :],
                                        ALU.mult)
                        v.tensor_tensor(rowA4[:, :], rowA4[:, :], rowB4[:, :],
                                        ALU.subtract)
                        v.tensor_scalar_add(rowA4[:, :], rowA4[:, :],
                                            1e-5).then_inc(sV.h, 1)
                    ph.raw("vector", v_r41)
                    sV.n += 1
                    ph.emit("scalar", "activation", rowB4[:, :], rowA4[:, :],
                            AF.Sqrt, waits=[(sV.h, sV.n)], tick=sA.tick())

                    def v_r42(v, w=sA.n):
                        v.wait_ge(sA.h, w)
                        v.reciprocal(rowA4[:, :], rowB4[:, :])
                        v.tensor_tensor(rowB4[:, :], rowM4[:, :], rowA4[:, :],
                                        ALU.mult)
                        v.tensor_scalar_mul(rowB4[:, :], rowB4[:, :],
                                            -1.0).then_inc(sV.h, 1)
                    ph.raw("vector", v_r42)
                    sV.n += 1
                    ph.emit("tensor", "matmul", brow(6, rows=128), ONESf[0:1, :],
                            rowA4[0:1, :], start=True, stop=True,
                            waits=[(sV.h, sV.n)], tick=sT.tick())
                    ph.emit("tensor", "matmul", brow(7, rows=128), ONESf[0:1, :],
                            rowB4[0:1, :], start=True, stop=True, tick=sT.tick())
                    ph.emit("scalar", "activation", Abc4[:, :], bank(6), AF.Copy,
                            waits=[(sT.h, sT.n)], tick=sA.tick())
                    ph.emit("scalar", "activation", Bbc4[:, :], bank(7), AF.Copy,
                            tick=sA.tick())
                    w_abc = sA.n

                    for chh in range(NCH):
                        b = chh % 2
                        ph.emit("scalar", "activation", xb4[:, b, :],
                                XT[:, chh, tsl], AF.Copy,
                                waits=[(sA.h, w_abc)] +
                                      ([fr_xb4[b]] if b in fr_xb4 else []),
                                tick=sA.tick())

                        def v_h4(v, b=b, w=sA.n, wtf=fr_tf4.get(b, 0)):
                            v.wait_ge(sA.h, max(w, wtf))
                            v.tensor_tensor(tf4[:, b, :], xb4[:, b, :], Abc4[:, :],
                                            ALU.mult)
                            v.tensor_tensor(tf4[:, b, :], tf4[:, b, :],
                                            Bbc4[:, :], ALU.add).then_inc(sV.h, 1)
                        ph.raw("vector", v_h4)
                        sV.n += 1
                        fr_xb4[b] = (sV.h, sV.n)
                        ph.emit("scalar", "activation", h2[:, chh, tsl],
                                tf4[:, b, :], AF.Identity,
                                scale=modcol[:, 64 + chh:65 + chh],
                                bias=modcol[:, 48 + chh:49 + chh],
                                waits=[(sV.h, sV.n)], tick=sA.tick())
                        fr_tf4[b] = sA.n
                    fr_stats4 = sA.n
                w_h2 = sA.n

                fr_pL = {}
                fr_tr = 0
                for t in range(8):
                    pb = t % 2

                    def t_log(te, t=t, pb=pb, w=w_h2 if t == 0 else 0,
                              wp=fr_pL.get(pb, 0)):
                        if w:
                            te.wait_ge(sA.h, w)
                        if wp:
                            te.wait_ge(sV.h, wp)
                        for chh in range(NCH):
                            te.matmul(brow(pb, rows=128, width=16),
                                      h2[:, chh, 128 * t:128 * (t + 1)],
                                      wr_sb[:, chh, :],
                                      start=(chh == 0), stop=False)
                        te.matmul(brow(pb, rows=128, width=16),
                                  ONESb[0:1, 0:128], fb_bf[0:1, :],
                                  start=False, stop=True).then_inc(sT.h, 1)
                    ph.raw("tensor", t_log)
                    sT.n += 1

                    def v_sm(v, pb=pb, w=sT.n):
                        v.wait_ge(sT.h, w)
                        pL = brow(pb, rows=128, width=16)
                        v.reduce_max(colA[:, :], pL, AX.X,
                                     negate=True).then_inc(sV.h, 1)
                    ph.raw("vector", v_sm)
                    sV.n += 1
                    ph.emit("scalar", "activation", ef[:, :],
                            brow(pb, rows=128, width=16), AF.Exp,
                            bias=colA[:, :], scale=1.0,
                            waits=[(sV.h, sV.n)], tick=sA.tick())

                    def v_sm2(v, w=sA.n):
                        v.wait_ge(sA.h, w)
                        v.reduce_sum(colB[:, :], ef[:, :], AX.X)
                        v.reciprocal(colB[:, :], colB[:, :])
                        v.tensor_scalar_mul(pf[:, :], ef[:, :], colB[:, :])
                        v.reduce_max(colA[:, :], pf[:, :], AX.X)
                        v.tensor_single_scalar(qf[:, :], pf[:, :], colA[:, :],
                                               ALU.is_ge)
                        v.scalar_tensor_tensor(qf[:, :], qf[:, :], -1e30,
                                               pf[:, :], ALU.mult, ALU.add)
                        v.reduce_max(colB[:, :], qf[:, :], AX.X)
                        v.tensor_tensor(colC[:, :], colA[:, :], colB[:, :], ALU.add)
                        v.tensor_scalar_add(colC[:, :], colC[:, :], 1e-9)
                        v.reciprocal(colC[:, :], colC[:, :])
                        v.tensor_single_scalar(qf[:, :], pf[:, :], colB[:, :],
                                               ALU.is_ge)
                        v.tensor_tensor(wfull[:, :], pf[:, :], qf[:, :], ALU.mult)
                        v.tensor_scalar_mul(wfull[:, :], wfull[:, :],
                                            colC[:, :]).then_inc(sV.h, 1)
                    ph.raw("vector", v_sm2)
                    sV.n += 1
                    fr_pL[pb] = sV.n

                    ph.emit("tensor", "transpose",
                            PS[0:16, 1024 + 128 * (t % 4):1024 + 128 * (t % 4) + 128],
                            wfull[:, :], idf[:, :],
                            waits=[(sV.h, sV.n), (sA.h, fr_tr)], tick=sT.tick())
                    ph.emit("scalar", "activation", wT[:, t, :],
                            PS[0:16, 1024 + 128 * (t % 4):1024 + 128 * (t % 4) + 128],
                            AF.Copy, waits=[(sT.h, sT.n)], tick=sA.tick())
                    if t % 4 == 3:
                        fr_tr = sA.n

                ph.emit("sync", "dma_start", out=wscr[:, :], in_=wT[:, :, :],
                        waits=[(sA.h, sA.n)], tick=sD.tick(16))
                w_wscr_out = sD.n

                if debug:
                    ph.emit("gpsimd", "dma_start", out=dbg["d_h2"][:, :, :],
                            in_=h2[:, :, :], waits=[(sA.h, sA.n)], tick=sO.tick(16))
                    ph.emit("sync", "dma_start", out=dbg["d_w"][:, :, :],
                            in_=wT[:, :, :], tick=sO.tick(16))
                ph.flush()

            with ExitStack() as p5ctx:
                e5 = p5ctx.enter_context
                AE = e5(nc.sbuf_tensor("AE", [128, 8, TO], BF16, side="right"))
                wgs = e5(nc.sbuf_tensor("wgs", [128, 2, 16, 128], BF16, side="right"))
                wus = e5(nc.sbuf_tensor("wus", [128, 2, 16, 128], BF16, side="right"))
                wds = e5(nc.sbuf_tensor("wds", [128, 2, 8, 128], BF16, side="right"))
                wrow = e5(nc.sbuf_tensor("wrow", [1, 2, TO], F32, side="right"))
                wbc = e5(nc.sbuf_tensor("wbc", [128, 2, 512], F32, side="right"))
                sgb = e5(nc.sbuf_tensor("sgb", [128, 2, 512], F32, side="right"))
                usb = e5(nc.sbuf_tensor("usb", [128, 2, 512], F32, side="right"))

                fr_wr = {}
                fr_wg = {}
                fr_wd = {}
                fr_g_ev = 0
                fr_u_ev = 0
                fr_sg = {}
                fr_ae_rd = 0
                fr_wbc_rd = 0
                fr_wb = 0
                fr_pr = {}
                for e in range(NE):
                    shared = (e == NE - 1)
                    eb = e % 2
                    if not shared:
                        ph.emit("sync", "dma_start", out=wrow[0:1, eb, :],
                                in_=wscr[e:e + 1, :],
                                waits=[(sD.h, w_wscr_out),
                                       (sA.h, fr_wr.get(eb, 0))],
                                tick=sD.tick(16))
                        w_wr = sD.n

                        def t_wbc(te, eb=eb, w=w_wr, wb=fr_g_ev):
                            te.wait_ge(sD.h, w)
                            if wb:
                                te.wait_ge(sA.h, wb)
                            for tau2 in range(2):
                                te.matmul(brow(tau2, rows=128), ONESf[0:1, :],
                                          wrow[0:1, eb,
                                               512 * tau2:512 * (tau2 + 1)],
                                          start=True,
                                          stop=True).then_inc(sT.h, 1)
                        ph.raw("tensor", t_wbc)
                        sT.n += 2

                        def a_wbc(sc, w=sT.n, wrd=fr_wbc_rd):
                            sc.wait_ge(sT.h, w)
                            if wrd:
                                sc.wait_ge(sV.h, wrd)
                            sc.activation(wbc[:, 0, :], bank(0),
                                          AF.Copy).then_inc(sA.h, 1)
                            sc.activation(wbc[:, 1, :], bank(1),
                                          AF.Copy).then_inc(sA.h, 1)
                        ph.raw("scalar", a_wbc)
                        sA.n += 2
                        fr_wr[eb] = sA.n
                        w_wbce = sA.n
                    else:
                        w_wbce = 0

                    for i in range(8):
                        gb_ = i % 2
                        ph.emit("gpsimd", "dma_start", out=wgs[:, gb_, :, :],
                                in_=wg_p[e, i, :, :, :],
                                waits=[(sT.h, fr_wg.get(gb_, 0))], tick=sG.tick(16))
                        ph.emit("gpsimd", "dma_start", out=wus[:, gb_, :, :],
                                in_=wu_p[e, i, :, :, :], tick=sG.tick(16))
                        w_wgu = sG.n

                        def t_gu(te, gb_=gb_, w=w_wgu, wg_ev=fr_g_ev,
                                 wu_ev=fr_u_ev, wrd=fr_ae_rd,
                                 wbce=(w_wbce if i == 0 else 0)):
                            te.wait_ge(sG.h, w)
                            if wg_ev:
                                te.wait_ge(sA.h, wg_ev)
                            if wbce:
                                te.wait_ge(sA.h, wbce)
                            if wu_ev:
                                te.wait_ge(sV.h, wu_ev)
                            if wrd:
                                te.wait_ge(sT.h, wrd)
                            for chh in range(NCH):
                                te.matmul(bank(0), wgs[:, gb_, chh, :],
                                          h2[:, chh, 0:512],
                                          start=(chh == 0), stop=(chh == 15))
                                te.matmul(bank(1), wgs[:, gb_, chh, :],
                                          h2[:, chh, 512:1024],
                                          start=(chh == 0), stop=(chh == 15))
                            for chh in range(NCH):
                                te.matmul(bank(2), wus[:, gb_, chh, :],
                                          h2[:, chh, 0:512],
                                          start=(chh == 0), stop=(chh == 15))
                                inst = te.matmul(bank(3), wus[:, gb_, chh, :],
                                                 h2[:, chh, 512:1024],
                                                 start=(chh == 0), stop=(chh == 15))
                            inst.then_inc(sT.h, 1)
                        ph.raw("tensor", t_gu)
                        sT.n += 1
                        fr_wg[gb_] = sT.n
                        w_gu = sT.n

                        for tau2 in range(2):
                            ph.emit("scalar", "activation", sgb[:, tau2, :],
                                    bank(tau2), AF.Silu,
                                    waits=[(sT.h, w_gu),
                                           (sV.h, fr_sg.get(tau2, 0))],
                                    tick=sA.tick())
                            w_sg = sA.n

                            def v_ae(v, i=i, tau2=tau2, shared=shared,
                                     w=w_sg, wb=w_wbce, wrd=fr_ae_rd):
                                v.wait_ge(sA.h, max(w, wb))
                                if wrd:
                                    v.wait_ge(sT.h, wrd)
                                if shared:
                                    v.tensor_tensor(
                                        AE[:, i, 512 * tau2:512 * (tau2 + 1)],
                                        bank(2 + tau2), sgb[:, tau2, :],
                                        ALU.mult).then_inc(sV.h, 1)
                                else:
                                    v.tensor_tensor(usb[:, tau2, :], bank(2 + tau2),
                                                    wbc[:, tau2, :], ALU.mult)
                                    v.tensor_tensor(
                                        AE[:, i, 512 * tau2:512 * (tau2 + 1)],
                                        usb[:, tau2, :], sgb[:, tau2, :],
                                        ALU.mult).then_inc(sV.h, 1)
                            ph.raw("vector", v_ae)
                            sV.n += 1
                            fr_sg[tau2] = sV.n
                        fr_g_ev = sA.n
                        fr_u_ev = sV.n
                    w_ae = sV.n
                    fr_wbc_rd = sV.n

                    for c in range(16):
                        db_ = c % 2
                        cb = 4 + 2 * (c % 2)
                        ph.emit("gpsimd", "dma_start", out=wds[:, db_, :, :],
                                in_=wd_p[e, c, :, :, :],
                                waits=[(sT.h, fr_wd.get(db_, 0))], tick=sG.tick(16))
                        w_wd = sG.n

                        def t_wd(te, db_=db_, cb=cb, w=w_wd,
                                 wae=w_ae if c == 0 else 0,
                                 wp=fr_pr.get(c % 2, 0)):
                            te.wait_ge(sG.h, w)
                            if wae:
                                te.wait_ge(sV.h, wae)
                            if wp:
                                te.wait_ge(sV.h, wp)
                            for k in range(8):
                                te.matmul(bank(cb), wds[:, db_, k, :],
                                          AE[:, k, 0:512],
                                          start=(k == 0), stop=(k == 7))
                                inst = te.matmul(bank(cb + 1), wds[:, db_, k, :],
                                                 AE[:, k, 512:1024],
                                                 start=(k == 0), stop=(k == 7))
                            inst.then_inc(sT.h, 1)
                        ph.raw("tensor", t_wd)
                        sT.n += 1
                        fr_wd[db_] = sT.n
                        fr_ae_rd = sT.n

                        for tau2 in range(2):
                            ph.emit("vector", "scalar_tensor_tensor",
                                    XT[:, c, 512 * tau2:512 * (tau2 + 1)],
                                    bank(cb + tau2), modcol[:, 80 + c:81 + c],
                                    XT[:, c, 512 * tau2:512 * (tau2 + 1)],
                                    ALU.mult, ALU.add,
                                    waits=[(sT.h, sT.n)], tick=sV.tick())
                        fr_pr[c % 2] = sV.n
                w_ffn = sV.n

                def s_out(s_, w=w_ffn):
                    s_.wait_ge(sV.h, w)
                    s_.dma_start(out=yT_d.rearrange("(c p) t -> p c t", p=128),
                                 in_=XT[:, :, :]).then_inc(sO.h, 16)
                ph.raw("sync", s_out)
                sO.n += 16
                fin = sO.n

                def s_fin(s_, w=fin):
                    s_.wait_ge(sO.h, w)
                ph.raw("sync", s_fin)
                ph.flush()

    return nc


def _host_layout(inputs):
    x = np.asarray(inputs["L0_attn_hidden_states"], dtype=np.float32)
    Wq = np.asarray(inputs["L0_attn_Wq"], dtype=np.float32)
    Wk = np.asarray(inputs["L0_attn_Wk"], dtype=np.float32)
    Wv = np.asarray(inputs["L0_attn_Wv"], dtype=np.float32)
    Wg = np.asarray(inputs["L0_attn_Wg"], dtype=np.float32)
    Wc = np.asarray(inputs["L0_attn_Wc"], dtype=np.float32)
    bc = np.asarray(inputs["L0_attn_bc"], dtype=np.float32)
    Wr = np.asarray(inputs["L0_ffn_Wr"], dtype=np.float32)
    fbias = np.asarray(inputs["L0_ffn_bias"], dtype=np.float32)
    Wge = np.asarray(inputs["L0_ffn_Wg"], dtype=np.float32)
    Wue = np.asarray(inputs["L0_ffn_Wu"], dtype=np.float32)
    Wde = np.asarray(inputs["L0_ffn_Wd"], dtype=np.float32)
    sWg = np.asarray(inputs["L0_ffn_sWg"], dtype=np.float32)
    sWu = np.asarray(inputs["L0_ffn_sWu"], dtype=np.float32)
    sWd = np.asarray(inputs["L0_ffn_sWd"], dtype=np.float32)
    cmat = np.asarray(inputs["L0_c"], dtype=np.float32)
    wmod = np.asarray(inputs["L0_wmod"], dtype=np.float32)

    def perm_lhs(W):
        ci, co = W.shape
        return np.ascontiguousarray(
            W.reshape(ci // 128, 128, co // 128, 128).transpose(2, 1, 0, 3))

    wq_pm = perm_lhs(Wq)
    wk_pm = perm_lhs(Wk)
    wc_pm = perm_lhs(Wc)
    wgat_pm = np.ascontiguousarray(Wg.reshape(16, 128, 8).transpose(1, 0, 2))
    wr_pm = np.ascontiguousarray(
        np.ascontiguousarray(Wr.T).reshape(16, 128, 16).transpose(1, 0, 2))
    bc_pm = np.ascontiguousarray(bc.reshape(16, 128).T)

    wg_all = np.concatenate([Wge, sWg[None]], axis=0)
    wu_all = np.concatenate([Wue, sWu[None]], axis=0)
    wd_all = np.concatenate([Wde, sWd[None]], axis=0)
    wg_pm = np.ascontiguousarray(
        wg_all.reshape(NE, 16, 128, 8, 128).transpose(0, 3, 2, 1, 4))
    wu_pm = np.ascontiguousarray(
        wu_all.reshape(NE, 16, 128, 8, 128).transpose(0, 3, 2, 1, 4))
    wd_pm = np.ascontiguousarray(
        wd_all.reshape(NE, 8, 128, 16, 128).transpose(0, 3, 2, 1, 4))

    shared = dict(wq_p=wq_pm, wk_p=wk_pm, wv_n=Wv, wc_p=wc_pm, wgat_p=wgat_pm,
                  wmod_n=wmod, wr_p=wr_pm, fbias=fbias.reshape(1, 16),
                  bc_p=bc_pm, wg_p=wg_pm, wu_p=wu_pm, wd_p=wd_pm)

    in_maps = []
    own_tok = {}
    for core in range(8):
        b = core // 2
        j = core % 2
        blocks = OWN_BLOCKS[j]
        toks = np.concatenate([np.arange(128 * bl, 128 * (bl + 1))
                               for bl in blocks])
        own_tok[core] = toks
        xb = x[b]
        xT = np.ascontiguousarray(xb.T)
        xTo = np.ascontiguousarray(xb.T[:, toks])
        csT = np.ascontiguousarray(cmat[b].reshape(16, 128).T)

        thrm = np.zeros((128, 32), dtype=np.float32)
        for tau in range(2):
            T0 = TILE_T0[j][tau]
            for I in range(16):
                if I in TILE_MASKED[tau]:
                    if 128 * I >= T0 + 512:
                        val = -1e9
                    else:
                        val = float(T0 - 128 * I)
                else:
                    val = 1e9
                thrm[:, 16 * tau + I] = val

        mrep = np.zeros((2, 8, T), dtype=np.float32)
        for r in range(2):
            start = TILE_T0[j][r]
            mrep[r, :, :start] = 1.0

        m = dict(shared)
        m.update(xT_full=xT, xT_own=xTo, csT=csT, thr=thrm, maskrep=mrep)
        in_maps.append(m)
    return in_maps, own_tok


_PROG_CACHE = {}


def kernel(**inputs):
    debug = bool(inputs.pop("_debug", False))
    in_maps, own_tok = _host_layout(inputs)
    key = ("prog", debug)
    if key not in _PROG_CACHE:
        _PROG_CACHE[key] = build_program(debug=debug)
    nc = _PROG_CACHE[key]
    res = run_bass_kernel_spmd(nc, in_maps, core_ids=list(range(8)))
    out = np.zeros((4, T, C), dtype=np.float32)
    dbg_out = []
    for core in range(8):
        b = core // 2
        r = res.results[core]
        out[b, own_tok[core], :] = np.asarray(r["yT"]).T
        if debug:
            dbg_out.append(r)
    if debug:
        kernel._dbg = dbg_out
    return out


# version 03
# speedup vs baseline: 1.1222x; 1.1222x over previous
import numpy as np
from contextlib import ExitStack

import concourse.bass as bass
import concourse.mybir as mybir
from concourse.bass_utils import run_bass_kernel_spmd

F32 = mybir.dt.float32
BF16 = mybir.dt.bfloat16
AF = mybir.ActivationFunctionType
ALU = mybir.AluOpType
AX = mybir.AxisListType

H = 8
NQH = 16
C = 2048
T = 2048
TO = 1024
NE = 17
NCH = 16
GATE_OFFSET = 6.906768
LOG_ATT_SCALE = float(np.log(1.0 / np.sqrt(128.0)))
LOG_EPS = float(np.log(1e-6))

OWN_BLOCKS = {0: [0, 1, 2, 3, 12, 13, 14, 15], 1: [4, 5, 6, 7, 8, 9, 10, 11]}
TILE_T0 = {0: [0, 1536], 1: [512, 1024]}
TILE_NCH = [8, 16]
TILE_MASKED = {0: list(range(0, 8)), 1: list(range(8, 16))}


class Sem:
    def __init__(self, ctx, nc, name):
        self.h = ctx.enter_context(nc.semaphore(name))
        self.n = 0

    def tick(self, k=1):
        self.n += k
        return (self.h, k)

    @property
    def at(self):
        return (self.h, self.n)


_DVE_DRAIN_OPS = {
    "tensor_tensor", "tensor_scalar", "tensor_scalar_add", "tensor_scalar_mul",
    "tensor_scalar_sub", "tensor_scalar_max", "tensor_scalar_min",
    "tensor_single_scalar", "tensor_copy", "tensor_add", "tensor_sub",
    "tensor_mul", "tensor_max", "reduce_sum", "reduce_max", "tensor_reduce",
    "reciprocal", "scalar_tensor_tensor", "tensor_tensor_scan", "select",
    "transpose", "memset", "iota", "tensor_relu",
}


class _VProxy:

    def __init__(self, eng):
        object.__setattr__(self, "_e", eng)

    def __getattr__(self, name):
        attr = getattr(object.__getattribute__(self, "_e"), name)
        if name in _DVE_DRAIN_OPS:
            eng = object.__getattribute__(self, "_e")

            def wrapped(*a, **k):
                inst = attr(*a, **k)
                eng.drain()
                return inst
            return wrapped
        return attr


class Phase:
    def __init__(self, nc):
        self.nc = nc
        self.q = {"tensor": [], "vector": [], "scalar": [], "gpsimd": [], "sync": []}

    def emit(self, eng, name, *args, waits=(), tick=None, ticks=(), **kwargs):
        waits = tuple((h, c) for (h, c) in waits if c > 0)
        allticks = tuple(t for t in ((tick,) if tick else ()) + tuple(ticks) if t)

        def fn(e, name=name, args=args, kwargs=kwargs, waits=waits,
               allticks=allticks):
            for h, c in waits:
                e.wait_ge(h, c)
            inst = getattr(e, name)(*args, **kwargs)
            for h, k in allticks:
                inst.then_inc(h, k)
        self.q[eng].append(fn)

    def raw(self, eng, fn):
        self.q[eng].append(fn)

    def flush(self):
        with self.nc.Block() as blk:
            for name, deco in (("sync", blk.sync), ("gpsimd", blk.gpsimd),
                               ("tensor", blk.tensor), ("vector", blk.vector),
                               ("scalar", blk.scalar)):
                ops = self.q[name]
                if ops:
                    if name == "vector":
                        deco(lambda eng, ops=ops: [fn(_VProxy(eng)) for fn in ops])
                    else:
                        deco(lambda eng, ops=ops: [fn(eng) for fn in ops])
        self.q = {k: [] for k in self.q}


def build_program(debug=False):
    nc = bass.Bass("TRN2", target_bir_lowering=False)

    def din(name, shape):
        return nc.dram_tensor(name, list(shape), F32, kind="ExternalInput").ap()

    xT_full = din("xT_full", (C, T))
    xT_own = din("xT_own", (C, TO))
    csT_d = din("csT", (128, 16))
    thr_d = din("thr", (128, 32))
    mrep_d = din("maskrep", (2, 8, T))
    wq_p = din("wq_p", (16, 128, 16, 128))
    wk_p = din("wk_p", (8, 128, 16, 128))
    wv_n = din("wv_n", (C, 1024))
    wc_p = din("wc_p", (16, 128, 16, 128))
    wgat_p = din("wgat_p", (128, 16, 8))
    wmod_n = din("wmod_n", (C, 12288))
    wr_p = din("wr_p", (128, 16, 16))
    fbias_d = din("fbias", (1, 16))
    bc_d = din("bc_p", (128, 16))
    wg_p = din("wg_p", (NE, 8, 128, 16, 128))
    wu_p = din("wu_p", (NE, 8, 128, 16, 128))
    wd_p = din("wd_p", (NE, 16, 128, 8, 128))

    yT_d = nc.dram_tensor("yT", [C, TO], F32, kind="ExternalOutput").ap()
    scr8 = nc.dram_tensor("scr8", [8, TO], F32, kind="Internal").ap()
    wscr = nc.dram_tensor("wscr", [16, TO], F32, kind="Internal").ap()

    dbg = {}
    if debug:
        for nm, shape in [("d_mod", (128, 96)), ("d_logG", (8, T)),
                          ("d_mrow", (1, 12288)), ("d_cs", (128, 16)),
                          ("d_scl", (128, 16, 8)), ("d_pref", (8, 2)),
                          ("d_k", (128, 8, T)), ("d_v", (128, 16, 1024)),
                          ("d_q", (128, 16, 512)), ("d_xmid", (128, NCH, TO)),
                          ("d_w", (16, 8, 128)), ("d_h2", (128, NCH, TO))]:
            dbg[nm] = nc.dram_tensor(nm, list(shape), F32, kind="ExternalOutput").ap()

    ctx = ExitStack()
    with ctx:
        en = ctx.enter_context

        XT = en(nc.sbuf_tensor("XT", [128, NCH, TO], F32, side="left"))
        iota = en(nc.sbuf_tensor("iota", [128, 512], F32, side="left"))
        ONESf = en(nc.sbuf_tensor("ONESf", [128, 128], F32, side="left"))
        ONESb = en(nc.sbuf_tensor("ONESb", [128, 128], BF16, side="left"))
        idf = en(nc.sbuf_tensor("idf", [128, 128], F32, side="left"))
        modcol = en(nc.sbuf_tensor("modcol", [128, 96], F32, side="left"))
        sc32 = en(nc.sbuf_tensor("sc32", [128, 32], F32, side="left"))
        gabc = en(nc.sbuf_tensor("gabc", [128, 16], F32, side="left"))
        sclCol = en(nc.sbuf_tensor("sclCol", [128, 16, 8], F32, side="left"))
        thr = en(nc.sbuf_tensor("thr_sb", [128, 32], F32, side="left"))
        bc_sb = en(nc.sbuf_tensor("bc_sb", [128, 16], F32, side="left"))
        csT_sb = en(nc.sbuf_tensor("csT_sb", [128, 16], F32, side="left"))
        csT_bf = en(nc.sbuf_tensor("csT_bf", [128, 16], BF16, side="left"))
        csT_f = en(nc.sbuf_tensor("csT_f", [128, 16], F32, side="left"))
        fb_sb = en(nc.sbuf_tensor("fb_sb", [1, 16], F32, side="left"))
        fb_bf = en(nc.sbuf_tensor("fb_bf", [1, 16], BF16, side="left"))
        wr_sb = en(nc.sbuf_tensor("wr_sb", [128, 16, 16], BF16, side="left"))
        wgat_sb = en(nc.sbuf_tensor("wgat_sb", [128, 16, 8], BF16, side="left"))
        prefx = en(nc.sbuf_tensor("prefx", [8, 2], F32, side="left"))
        cst8 = en(nc.sbuf_tensor("cst8", [8, 4], F32, side="left"))

        PS = en(nc.psum_tensor("PS", [128, 4096], F32))

        def bank(k):
            return PS[:, 512 * k:512 * (k + 1)]

        def brow(k, rows=1, width=512, off=0):
            return PS[0:rows, 512 * k + off:512 * k + off + width]

        sD = Sem(ctx, nc, "sD")
        sG = Sem(ctx, nc, "sG")
        sT = Sem(ctx, nc, "sT")
        sV = Sem(ctx, nc, "sV")
        sA = Sem(ctx, nc, "sA")
        sO = Sem(ctx, nc, "sO")

        ph = Phase(nc)

        with nc.sbuf_tensor("wslf", [128, 4, 512], F32, side="right") as wslf, \
             nc.sbuf_tensor("mrow", [1, 12288], F32, side="right") as mrow:

            ph.emit("gpsimd", "iota", iota[:, :], [[-1, 512]], channel_multiplier=1,
                    allow_small_or_imprecise_dtypes=True)
            ph.emit("gpsimd", "memset", ONESf[:, :], 1.0)
            ph.emit("gpsimd", "memset", ONESb[:, :], 1.0)
            ph.emit("gpsimd", "memset", cst8[:, 0:1], -GATE_OFFSET)
            ph.emit("gpsimd", "memset", cst8[:, 1:2], LOG_ATT_SCALE)
            ph.emit("gpsimd", "memset", cst8[:, 2:3], LOG_EPS, tick=sG.tick())
            w_init = sG.n
            ph.emit("gpsimd", "dma_start", out=wr_sb[:, :, :], in_=wr_p[:, :, :],
                    tick=sG.tick(16))
            ph.emit("gpsimd", "dma_start", out=wgat_sb[:, :, :], in_=wgat_p[:, :, :],
                    tick=sG.tick(16))

            ph.emit("sync", "dma_start", out=csT_sb[:, :], in_=csT_d[:, :],
                    tick=sD.tick(16))
            ph.emit("sync", "dma_start", out=thr[:, :], in_=thr_d[:, :],
                    tick=sD.tick(16))
            ph.emit("sync", "dma_start", out=bc_sb[:, :], in_=bc_d[:, :],
                    tick=sD.tick(16))
            ph.emit("sync", "dma_start", out=fb_sb[:, :], in_=fbias_d[:, :],
                    tick=sD.tick(16))
            w_small = sD.n

            ph.emit("vector", "tensor_single_scalar", idf[:, :], iota[:, 0:128],
                    0.0, ALU.is_equal, waits=[(sG.h, w_init)], tick=sV.tick())

            ph.emit("scalar", "activation", csT_f[:, :], csT_sb[:, :], AF.Silu,
                    waits=[(sD.h, w_small)], tick=sA.tick())
            ph.emit("scalar", "activation", fb_bf[:, :], fb_sb[:, :], AF.Copy,
                    tick=sA.tick())
            w_silu = sA.n

            fr_buf = {}
            fr_ps = {}
            for t in range(24):
                for s2 in range(NCH):
                    b = (t * NCH + s2) % 4
                    ph.emit("sync", "dma_start",
                            out=wslf[:, b, :],
                            in_=wmod_n[128 * s2:128 * (s2 + 1),
                                       512 * t:512 * (t + 1)],
                            waits=[(sT.h, fr_buf.get(b, 0))], tick=sD.tick(16))
                    w_dma = sD.n
                    waits = [(sD.h, w_dma), (sA.h, w_silu)]
                    if s2 == 0:
                        waits.append((sV.h, fr_ps.get(t % 2, 0)))
                    ph.emit("tensor", "matmul", brow(t % 2), csT_f[:, s2:s2 + 1],
                            wslf[:, b, :], start=(s2 == 0), stop=(s2 == 15),
                            waits=waits, tick=sT.tick())
                    fr_buf[b] = sT.n
                ph.emit("vector", "tensor_copy", mrow[0:1, 512 * t:512 * (t + 1)],
                        brow(t % 2), waits=[(sT.h, sT.n)], tick=sV.tick())
                fr_ps[t % 2] = sV.n

            if debug:
                ph.emit("sync", "dma_start", out=dbg["d_mrow"][:, :],
                        in_=mrow[0:1, :], waits=[(sV.h, sV.n)], tick=sO.tick(16))
                ph.emit("sync", "dma_start", out=dbg["d_cs"][:, :],
                        in_=csT_f[:, :], waits=[(sA.h, sA.n)], tick=sO.tick(16))

            def t_cols(te, w=sV.n):
                te.wait_ge(sV.h, w)
                for jj in range(96):
                    inst = te.matmul(PS[:, 1024 + jj:1024 + jj + 1],
                                     mrow[0:1, 128 * jj:128 * (jj + 1)],
                                     ONESf[0:1, 0:1],
                                     start=(jj == 0), stop=(jj == 95))
                inst.then_inc(sT.h, 1)
            ph.raw("tensor", t_cols)
            sT.n += 1

            ph.emit("scalar", "activation", modcol[:, :], PS[:, 1024:1120], AF.Copy,
                    waits=[(sT.h, sT.n)], tick=sA.tick())

            def v_clamp(v, w=sA.n):
                v.wait_ge(sA.h, w)
                for base in (32, 80):
                    sl = modcol[:, base:base + 16]
                    v.tensor_single_scalar(sc32[:, 0:16], sl, 0.0, ALU.is_ge)
                    v.tensor_scalar(sc32[:, 0:16], sc32[:, 0:16], 2.0, -1.0,
                                    ALU.mult, ALU.add)
                    v.tensor_scalar_mul(sc32[:, 16:32], sl, -1.0)
                    v.tensor_tensor(sc32[:, 16:32], sl, sc32[:, 16:32], ALU.max)
                    v.tensor_scalar_max(sc32[:, 16:32], sc32[:, 16:32], 0.25)
                    v.tensor_tensor(sl, sc32[:, 0:16], sc32[:, 16:32], ALU.mult)
                v.tensor_scalar_add(modcol[:, 16:32], modcol[:, 16:32], 1.0)
                v.tensor_scalar_add(modcol[:, 64:80], modcol[:, 64:80], 1.0)
                v.tensor_tensor(gabc[:, :], modcol[:, 32:48], bc_sb[:, :],
                                ALU.mult).then_inc(sV.h, 1)
            ph.raw("vector", v_clamp)
            sV.n += 1

            if debug:
                ph.emit("sync", "dma_start", out=dbg["d_mod"][:, :],
                        in_=modcol[:, :], waits=[(sV.h, sV.n)], tick=sO.tick(16))
            ph.flush()

        with nc.sbuf_tensor("KB", [128, H, T], BF16, side="right") as KB, \
             nc.sbuf_tensor("VB", [128, 16, 1024], BF16, side="right") as VB, \
             nc.sbuf_tensor("gG", [8, T], F32, side="right") as gG:

            with ExitStack() as p1ctx:
                e1 = p1ctx.enter_context
                HTt = e1(nc.sbuf_tensor("HTt", [128, NCH, 512], BF16, side="right"))
                xs1 = e1(nc.sbuf_tensor("xs1", [128, 2, 512], F32, side="right"))
                xb1 = e1(nc.sbuf_tensor("xb1", [128, 2, 512], BF16, side="right"))
                sq1 = e1(nc.sbuf_tensor("sq1", [128, 2, 512], BF16, side="right"))
                tf1 = e1(nc.sbuf_tensor("tf1", [128, 2, 512], F32, side="right"))
                rowM = e1(nc.sbuf_tensor("rowM", [1, 512], F32, side="right"))
                rowA = e1(nc.sbuf_tensor("rowA", [1, 512], F32, side="right"))
                rowB = e1(nc.sbuf_tensor("rowB", [1, 512], F32, side="right"))
                Abc = e1(nc.sbuf_tensor("Abc", [128, 512], F32, side="right"))
                Bbc = e1(nc.sbuf_tensor("Bbc", [128, 512], F32, side="right"))
                wk2 = e1(nc.sbuf_tensor("wk2", [128, 2, 16, 128], BF16, side="right"))
                wv2 = e1(nc.sbuf_tensor("wv2", [128, 2, 1024], BF16, side="right"))

                fr_xs = {}
                fr_xb = {}
                fr_tf = {}
                fr_wk = {}
                fr_wv = {}
                fr_kbank = {}
                ht_at = {}
                w_stats_free = 0
                for tg in range(4):
                    gsl = slice(512 * tg, 512 * (tg + 1))
                    for chh in range(NCH):
                        b = chh % 2
                        ph.emit("sync", "dma_start", out=xs1[:, b, :],
                                in_=xT_full[128 * chh:128 * (chh + 1), gsl],
                                waits=[(sA.h, fr_xs.get(b, 0))], tick=sD.tick(16))
                        ph.emit("scalar", "activation", xb1[:, b, :], xs1[:, b, :],
                                AF.Copy,
                                waits=[(sD.h, sD.n)] +
                                      ([fr_xb[b]] if b in fr_xb else []),
                                tick=sA.tick())
                        ph.emit("scalar", "activation", sq1[:, b, :], xs1[:, b, :],
                                AF.Square, tick=sA.tick())
                        fr_xs[b] = sA.n
                        ph.emit("tensor", "matmul", bank(4), ONESb[:, 0:128],
                                xb1[:, b, :], start=(chh == 0), stop=(chh == 15),
                                waits=[(sA.h, sA.n)] +
                                      ([(sA.h, w_stats_free)] if chh == 0 else []),
                                tick=sT.tick())
                        ph.emit("tensor", "matmul", bank(5), ONESb[:, 0:128],
                                sq1[:, b, :], start=(chh == 0), stop=(chh == 15),
                                tick=sT.tick())
                        fr_xb[b] = (sT.h, sT.n)

                    def v_rows1(v, w=sT.n):
                        v.wait_ge(sT.h, w)
                        v.tensor_scalar_mul(rowM[:, :], brow(4), 1.0 / 2048.0)
                        v.tensor_scalar_mul(rowA[:, :], brow(5), 1.0 / 2048.0)
                        v.tensor_tensor(rowB[:, :], rowM[:, :], rowM[:, :], ALU.mult)
                        v.tensor_tensor(rowA[:, :], rowA[:, :], rowB[:, :],
                                        ALU.subtract)
                        v.tensor_scalar_add(rowA[:, :], rowA[:, :],
                                            1e-5).then_inc(sV.h, 1)
                    ph.raw("vector", v_rows1)
                    sV.n += 1
                    ph.emit("scalar", "activation", rowB[:, :], rowA[:, :], AF.Sqrt,
                            waits=[(sV.h, sV.n)], tick=sA.tick())

                    def v_rows2(v, w=sA.n):
                        v.wait_ge(sA.h, w)
                        v.reciprocal(rowA[:, :], rowB[:, :])
                        v.tensor_tensor(rowB[:, :], rowM[:, :], rowA[:, :], ALU.mult)
                        v.tensor_scalar_mul(rowB[:, :], rowB[:, :],
                                            -1.0).then_inc(sV.h, 1)
                    ph.raw("vector", v_rows2)
                    sV.n += 1
                    ph.emit("tensor", "matmul", brow(6, rows=128), ONESf[0:1, :],
                            rowA[0:1, :], start=True, stop=True,
                            waits=[(sV.h, sV.n)], tick=sT.tick())
                    ph.emit("tensor", "matmul", brow(7, rows=128), ONESf[0:1, :],
                            rowB[0:1, :], start=True, stop=True, tick=sT.tick())
                    ph.emit("scalar", "activation", Abc[:, :], bank(6), AF.Copy,
                            waits=[(sT.h, sT.n)], tick=sA.tick())
                    ph.emit("scalar", "activation", Bbc[:, :], bank(7), AF.Copy,
                            tick=sA.tick())
                    w_abc = sA.n

                    for chh in range(NCH):
                        b = chh % 2
                        ph.emit("sync", "dma_start", out=xs1[:, b, :],
                                in_=xT_full[128 * chh:128 * (chh + 1), gsl],
                                waits=[(sA.h, fr_xs.get(b, 0))], tick=sD.tick(16))
                        ph.emit("scalar", "activation", xb1[:, b, :], xs1[:, b, :],
                                AF.Copy,
                                waits=[(sD.h, sD.n)] +
                                      ([fr_xb[b]] if b in fr_xb else []),
                                tick=sA.tick())
                        fr_xs[b] = sA.n

                        def v_h(v, b=b, w=max(sA.n, w_abc), wtf=fr_tf.get(b, 0)):
                            v.wait_ge(sA.h, max(w, wtf))
                            v.tensor_tensor(tf1[:, b, :], xb1[:, b, :], Abc[:, :],
                                            ALU.mult)
                            v.tensor_tensor(tf1[:, b, :], tf1[:, b, :], Bbc[:, :],
                                            ALU.add).then_inc(sV.h, 1)
                        ph.raw("vector", v_h)
                        sV.n += 1
                        fr_xb[b] = (sV.h, sV.n)
                        ph.emit("scalar", "activation", HTt[:, chh, :], tf1[:, b, :],
                                AF.Identity,
                                scale=modcol[:, 16 + chh:17 + chh],
                                bias=modcol[:, chh:chh + 1],
                                waits=[(sV.h, sV.n)], tick=sA.tick())
                        fr_tf[b] = sA.n
                        ht_at[chh] = sA.n
                    w_ht = sA.n

                    for m in range(H):
                        b = m % 2
                        ph.emit("gpsimd", "dma_start", out=wk2[:, b, :, :],
                                in_=wk_p[m, :, :, :],
                                waits=[(sT.h, fr_wk.get(b, 0))], tick=sG.tick(16))

                        def t_k(te, m=m, b=b, wg=sG.n,
                                wa=fr_kbank.get(b, 0), hta=dict(ht_at)):
                            te.wait_ge(sG.h, wg)
                            if wa:
                                te.wait_ge(sA.h, wa)
                            for chh in range(NCH):
                                if m == 0:
                                    te.wait_ge(sA.h, hta[chh])
                                inst = te.matmul(bank(b), wk2[:, b, chh, :],
                                                 HTt[:, chh, :],
                                                 start=(chh == 0), stop=(chh == 15))
                            inst.then_inc(sT.h, 1)
                        ph.raw("tensor", t_k)
                        sT.n += 1
                        fr_wk[b] = sT.n
                        ph.emit("scalar", "activation", KB[:, m, gsl], bank(b),
                                AF.Copy, waits=[(sT.h, sT.n)], tick=sA.tick())
                        fr_kbank[b] = sA.n

                    def t_g(te, w=sA.n):
                        te.wait_ge(sA.h, w)
                        for chh in range(NCH):
                            inst = te.matmul(brow(2, rows=8), wgat_sb[:, chh, :],
                                             HTt[:, chh, :],
                                             start=(chh == 0), stop=(chh == 15))
                        inst.then_inc(sT.h, 1)
                    ph.raw("tensor", t_g)
                    sT.n += 1
                    ph.emit("scalar", "activation", gG[0:8, gsl], brow(2, rows=8),
                            AF.Copy, waits=[(sT.h, sT.n)], tick=sA.tick())
                    w_g = sA.n

                    for chh in range(NCH):
                        b = chh % 2
                        ph.emit("gpsimd", "dma_start", out=wv2[:, b, :],
                                in_=wv_n[128 * chh:128 * (chh + 1), :],
                                waits=[(sT.h, fr_wv.get(b, 0))], tick=sG.tick(16))

                        def t_v(te, chh=chh, b=b, wg=sG.n, wa=w_g):
                            te.wait_ge(sG.h, wg)
                            if chh == 0:
                                te.wait_ge(sA.h, wa)
                            for tc in range(4):
                                for nt in range(2):
                                    inst = te.matmul(
                                        bank(tc * 2 + nt),
                                        HTt[:, chh, 128 * tc:128 * (tc + 1)],
                                        wv2[:, b, 512 * nt:512 * (nt + 1)],
                                        start=(chh == 0), stop=(chh == 15))
                            inst.then_inc(sT.h, 1)
                        ph.raw("tensor", t_v)
                        sT.n += 1
                        fr_wv[b] = sT.n
                    w_vmm = sT.n

                    for tc in range(4):
                        for nt in range(2):
                            ph.emit("scalar", "activation",
                                    VB[:, 4 * tg + tc, 512 * nt:512 * (nt + 1)],
                                    bank(tc * 2 + nt), AF.Copy,
                                    waits=[(sT.h, w_vmm)], tick=sA.tick())
                    w_stats_free = sA.n

                if debug:
                    ph.emit("gpsimd", "dma_start", out=dbg["d_k"][:, :, :],
                            in_=KB[:, :, :], waits=[(sA.h, sA.n)], tick=sO.tick(16))
                    ph.emit("gpsimd", "dma_start", out=dbg["d_v"][:, :, :],
                            in_=VB[:, :, :], tick=sO.tick(16))
                ph.flush()

            with nc.sbuf_tensor("gX", [8, T], F32, side="right") as gX, \
                 nc.sbuf_tensor("gHh", [8, T], F32, side="right") as gHh, \
                 nc.sbuf_tensor("gCum", [8, T], F32, side="right") as gCum, \
                 nc.sbuf_tensor("gZ", [8, T], F32, side="right") as gZ, \
                 nc.sbuf_tensor("mr_sb", [8, T], F32, side="right") as mr_sb:

                ph.emit("gpsimd", "memset", gZ[:, :], 0.0, tick=sG.tick())
                w_z = sG.n
                ph.emit("scalar", "activation", gX[:, :], gG[:, :], AF.Exp,
                        bias=cst8[:, 0:1], scale=-1.0, tick=sA.tick())

                def v_series(v, w=sA.n, wz=w_z):
                    v.wait_ge(sA.h, w)
                    v.wait_ge(sG.h, wz)
                    v.tensor_scalar(gHh[:, :], gX[:, :], -1.0 / 3.0, 0.5,
                                    ALU.mult, ALU.add)
                    v.tensor_tensor(gHh[:, :], gHh[:, :], gX[:, :], ALU.mult)
                    v.tensor_scalar_add(gHh[:, :], gHh[:, :], -1.0)
                    v.tensor_tensor(gHh[:, :], gHh[:, :], gX[:, :], ALU.mult)
                    v.tensor_tensor_scan(gCum[:, :], gHh[:, :], gZ[:, :],
                                         0.0, ALU.add, ALU.add).then_inc(sV.h, 1)
                ph.raw("vector", v_series)
                sV.n += 1

                for r in range(2):
                    ph.emit("sync", "dma_start", out=mr_sb[:, :],
                            in_=mrep_d[r, :, :],
                            waits=[(sV.h, sV.n if r else 0)], tick=sD.tick(16))

                    def v_pref(v, r=r, w=sD.n):
                        v.wait_ge(sD.h, w)
                        v.tensor_tensor(gX[:, :], gHh[:, :], mr_sb[:, :], ALU.mult)
                        v.reduce_sum(prefx[:, r:r + 1], gX[:, :],
                                     AX.X).then_inc(sV.h, 1)
                    ph.raw("vector", v_pref)
                    sV.n += 1

                ph.emit("scalar", "activation", gHh[:, :], gCum[:, :], AF.Exp,
                        bias=cst8[:, 1:2], scale=-0.5,
                        waits=[(sV.h, sV.n)], tick=sA.tick())

                def t_tr(te, w=sA.n):
                    te.wait_ge(sA.h, w)
                    for cc in range(16):
                        inst = te.transpose(PS[:, 8 * cc:8 * (cc + 1)],
                                            gHh[0:8, 128 * cc:128 * (cc + 1)],
                                            idf[0:8, 0:8])
                    inst.then_inc(sT.h, 1)
                ph.raw("tensor", t_tr)
                sT.n += 1
                ph.emit("scalar", "activation", sclCol[:, :, :], PS[:, 0:128],
                        AF.Copy, waits=[(sT.h, sT.n)], tick=sA.tick())

                if debug:
                    ph.emit("sync", "dma_start", out=dbg["d_logG"][:, :],
                            in_=gCum[:, :], waits=[(sA.h, sA.n)], tick=sO.tick(16))
                    ph.emit("sync", "dma_start", out=dbg["d_scl"][:, :, :],
                            in_=sclCol[:, :, :], tick=sO.tick(16))
                    ph.emit("sync", "dma_start", out=dbg["d_pref"][:, :],
                            in_=prefx[:, :], tick=sO.tick(16))
                ph.flush()

            with ExitStack() as p3ctx:
                e3 = p3ctx.enter_context
                QTt = e3(nc.sbuf_tensor("QTt", [128, 16, 512], BF16, side="right"))
                HOy = e3(nc.sbuf_tensor("HOy", [128, 16, 512], BF16, side="right"))
                ws2 = e3(nc.sbuf_tensor("ws2", [128, 2, 16, 128], BF16, side="right"))
                xb3 = e3(nc.sbuf_tensor("xb3", [128, 2, 512], BF16, side="right"))
                sq3 = e3(nc.sbuf_tensor("sq3", [128, 2, 512], BF16, side="right"))
                tf3 = e3(nc.sbuf_tensor("tf3", [128, 2, 512], F32, side="right"))
                rowM3 = e3(nc.sbuf_tensor("rowM3", [1, 512], F32, side="right"))
                rowA3 = e3(nc.sbuf_tensor("rowA3", [1, 512], F32, side="right"))
                rowB3 = e3(nc.sbuf_tensor("rowB3", [1, 512], F32, side="right"))
                Abc3 = e3(nc.sbuf_tensor("Abc3", [128, 512], F32, side="right"))
                Bbc3 = e3(nc.sbuf_tensor("Bbc3", [128, 512], F32, side="right"))
                go1 = e3(nc.sbuf_tensor("go1", [8, 512], F32, side="right"))
                go2 = e3(nc.sbuf_tensor("go2", [8, 512], F32, side="right"))
                goz = e3(nc.sbuf_tensor("goz", [8, 512], F32, side="right"))
                Ab2 = e3(nc.sbuf_tensor("Ab2", [128, 2, 512], BF16, side="right"))

                ph.emit("sync", "dma_start", out=XT[:, :, :],
                        in_=xT_own.rearrange("(c p) t -> p c t", p=128),
                        tick=sD.tick(16))
                w_xt = sD.n
                ph.emit("gpsimd", "memset", goz[:, :], 0.0, tick=sG.tick())
                w_z3 = sG.n

                fr_xb3 = {}
                fr_tf3 = {}
                fr_ws = {}
                fr_qbank = {}
                ho_at = {}
                fr_stats3 = 0
                for tau in range(2):
                    tsl = slice(512 * tau, 512 * tau + 512)
                    for chh in range(NCH):
                        b = chh % 2
                        ph.emit("scalar", "activation", xb3[:, b, :],
                                XT[:, chh, tsl], AF.Copy,
                                waits=[(sD.h, w_xt)] +
                                      ([fr_xb3[b]] if b in fr_xb3 else []),
                                tick=sA.tick())
                        ph.emit("scalar", "activation", sq3[:, b, :],
                                XT[:, chh, tsl], AF.Square, tick=sA.tick())
                        ph.emit("tensor", "matmul", bank(4), ONESb[:, 0:128],
                                xb3[:, b, :], start=(chh == 0), stop=(chh == 15),
                                waits=[(sA.h, sA.n)] +
                                      ([(sA.h, fr_stats3)] if chh == 0 else []),
                                tick=sT.tick())
                        ph.emit("tensor", "matmul", bank(5), ONESb[:, 0:128],
                                sq3[:, b, :], start=(chh == 0), stop=(chh == 15),
                                tick=sT.tick())
                        fr_xb3[b] = (sT.h, sT.n)

                    def v_r31(v, w=sT.n):
                        v.wait_ge(sT.h, w)
                        v.tensor_scalar_mul(rowM3[:, :], brow(4), 1.0 / 2048.0)
                        v.tensor_scalar_mul(rowA3[:, :], brow(5), 1.0 / 2048.0)
                        v.tensor_tensor(rowB3[:, :], rowM3[:, :], rowM3[:, :],
                                        ALU.mult)
                        v.tensor_tensor(rowA3[:, :], rowA3[:, :], rowB3[:, :],
                                        ALU.subtract)
                        v.tensor_scalar_add(rowA3[:, :], rowA3[:, :],
                                            1e-5).then_inc(sV.h, 1)
                    ph.raw("vector", v_r31)
                    sV.n += 1
                    ph.emit("scalar", "activation", rowB3[:, :], rowA3[:, :],
                            AF.Sqrt, waits=[(sV.h, sV.n)], tick=sA.tick())

                    def v_r32(v, w=sA.n):
                        v.wait_ge(sA.h, w)
                        v.reciprocal(rowA3[:, :], rowB3[:, :])
                        v.tensor_tensor(rowB3[:, :], rowM3[:, :], rowA3[:, :],
                                        ALU.mult)
                        v.tensor_scalar_mul(rowB3[:, :], rowB3[:, :],
                                            -1.0).then_inc(sV.h, 1)
                    ph.raw("vector", v_r32)
                    sV.n += 1
                    ph.emit("tensor", "matmul", brow(6, rows=128), ONESf[0:1, :],
                            rowA3[0:1, :], start=True, stop=True,
                            waits=[(sV.h, sV.n)], tick=sT.tick())
                    ph.emit("tensor", "matmul", brow(7, rows=128), ONESf[0:1, :],
                            rowB3[0:1, :], start=True, stop=True, tick=sT.tick())
                    ph.emit("scalar", "activation", Abc3[:, :], bank(6), AF.Copy,
                            waits=[(sT.h, sT.n)], tick=sA.tick())
                    ph.emit("scalar", "activation", Bbc3[:, :], bank(7), AF.Copy,
                            tick=sA.tick())
                    w_abc = sA.n

                    for chh in range(NCH):
                        b = chh % 2
                        ph.emit("scalar", "activation", xb3[:, b, :],
                                XT[:, chh, tsl], AF.Copy,
                                waits=[(sA.h, w_abc)] +
                                      ([fr_xb3[b]] if b in fr_xb3 else []),
                                tick=sA.tick())

                        def v_h3(v, b=b, w=sA.n, wtf=fr_tf3.get(b, 0)):
                            v.wait_ge(sA.h, max(w, wtf))
                            v.tensor_tensor(tf3[:, b, :], xb3[:, b, :], Abc3[:, :],
                                            ALU.mult)
                            v.tensor_tensor(tf3[:, b, :], tf3[:, b, :],
                                            Bbc3[:, :], ALU.add).then_inc(sV.h, 1)
                        ph.raw("vector", v_h3)
                        sV.n += 1
                        fr_xb3[b] = (sV.h, sV.n)
                        ph.emit("scalar", "activation", HOy[:, chh, :], tf3[:, b, :],
                                AF.Identity,
                                scale=modcol[:, 16 + chh:17 + chh],
                                bias=modcol[:, chh:chh + 1],
                                waits=[(sV.h, sV.n)], tick=sA.tick())
                        fr_tf3[b] = sA.n
                        ho_at[chh] = sA.n
                    w_ho = sA.n

                    def t_go(te, hoa=dict(ho_at)):
                        for chh in range(NCH):
                            te.wait_ge(sA.h, hoa[chh])
                            inst = te.matmul(brow(2, rows=8), wgat_sb[:, chh, :],
                                             HOy[:, chh, :],
                                             start=(chh == 0), stop=(chh == 15))
                        inst.then_inc(sT.h, 1)
                    ph.raw("tensor", t_go)
                    sT.n += 1
                    ph.emit("scalar", "activation", go1[:, :], brow(2, rows=8),
                            AF.Exp, bias=cst8[:, 0:1], scale=-1.0,
                            waits=[(sT.h, sT.n)], tick=sA.tick())

                    def v_gos(v, tau=tau, w=sA.n, wz=w_z3):
                        v.wait_ge(sA.h, w)
                        v.wait_ge(sG.h, wz)
                        v.tensor_scalar(go2[:, :], go1[:, :], -1.0 / 3.0, 0.5,
                                        ALU.mult, ALU.add)
                        v.tensor_tensor(go2[:, :], go2[:, :], go1[:, :], ALU.mult)
                        v.tensor_scalar_add(go2[:, :], go2[:, :], -1.0)
                        v.tensor_tensor(go2[:, :], go2[:, :], go1[:, :], ALU.mult)
                        v.tensor_tensor_scan(go1[:, :], go2[:, :], goz[:, :],
                                             0.0, ALU.add, ALU.add)
                        v.tensor_scalar_add(go1[:, :], go1[:, :],
                                            prefx[:, tau:tau + 1]).then_inc(sV.h, 1)
                    ph.raw("vector", v_gos)
                    sV.n += 1
                    ph.emit("scalar", "activation", go2[:, :], go1[:, :], AF.Exp,
                            bias=cst8[:, 2:3], scale=-1.0,
                            waits=[(sV.h, sV.n)], tick=sA.tick())
                    ph.emit("sync", "dma_start", out=scr8[:, tsl], in_=go2[:, :],
                            waits=[(sA.h, sA.n)], tick=sD.tick(16))
                    w_eps_out = sD.n

                    for m in range(16):
                        b = m % 2
                        ph.emit("gpsimd", "dma_start", out=ws2[:, b, :, :],
                                in_=wq_p[m, :, :, :],
                                waits=[(sT.h, fr_ws.get(b, 0))], tick=sG.tick(16))

                        def t_q(te, m=m, b=b, wg=sG.n,
                                wa=fr_qbank.get(b, 0), hoa=dict(ho_at)):
                            te.wait_ge(sG.h, wg)
                            if wa:
                                te.wait_ge(sA.h, wa)
                            for chh in range(NCH):
                                if m == 0:
                                    te.wait_ge(sA.h, hoa[chh])
                                inst = te.matmul(bank(b), ws2[:, b, chh, :],
                                                 HOy[:, chh, :],
                                                 start=(chh == 0), stop=(chh == 15))
                            inst.then_inc(sT.h, 1)
                        ph.raw("tensor", t_q)
                        sT.n += 1
                        fr_ws[b] = sT.n
                        ph.emit("scalar", "activation", QTt[:, m, :], bank(b),
                                AF.Copy, waits=[(sT.h, sT.n)], tick=sA.tick())
                        fr_qbank[b] = sA.n
                    w_q = sA.n

                    def v_gabc(v, tau=tau, wq=w_q):
                        v.wait_ge(sA.h, wq)
                        for chh in range(NCH):
                            v.tensor_scalar_add(
                                XT[:, chh, 512 * tau:512 * tau + 512],
                                XT[:, chh, 512 * tau:512 * tau + 512],
                                gabc[:, chh:chh + 1])
                        v.nop().then_inc(sV.h, 1)
                    ph.raw("vector", v_gabc)
                    sV.n += 1
                    w_pre = sV.n

                    nI = TILE_NCH[tau]
                    fr_eps = {}
                    fr_pS = {}
                    fr_Ab = {}
                    fr_pY = 0
                    fr_pN = 0
                    fr_rnb = 0
                    for h in range(NQH):
                        kv = h // 2
                        eb = h % 2
                        ph.emit("sync", "dma_start", out=(rowA3 if eb == 0 else rowB3)[0:1, :],
                                in_=scr8[kv:kv + 1, tsl],
                                waits=[(sD.h, w_eps_out),
                                       (sV.h, fr_eps.get(eb, 0))],
                                tick=sD.tick(16))
                        w_er = sD.n

                        for I in range(nI):
                            pb = I % 2
                            ph.emit("tensor", "matmul", bank(pb),
                                    KB[:, kv, 128 * I:128 * (I + 1)],
                                    QTt[:, h, :], start=True, stop=True,
                                    waits=([(sA.h, w_q)] if I == 0 else []) +
                                          [(sA.h, fr_pS.get(pb, 0))],
                                    tick=sT.tick())
                            ph.emit("scalar", "activation", Ab2[:, pb, :], bank(pb),
                                    AF.Square, scale=sclCol[:, I, kv:kv + 1],
                                    waits=[(sT.h, sT.n),
                                           (sT.h, fr_Ab.get(pb, 0))],
                                    tick=sA.tick())
                            fr_pS[pb] = sA.n
                            if I in TILE_MASKED[tau]:
                                ph.emit("vector", "scalar_tensor_tensor",
                                        Ab2[:, pb, :], iota[:, :],
                                        thr[:, 16 * tau + I:16 * tau + I + 1],
                                        Ab2[:, pb, :], ALU.is_le, ALU.mult,
                                        waits=[(sA.h, sA.n)], tick=sV.tick())
                                w_ready = [(sV.h, sV.n)]
                            else:
                                w_ready = [(sA.h, sA.n)]
                            ph.emit("tensor", "matmul", bank(2),
                                    VB[:, I, 128 * kv:128 * (kv + 1)],
                                    Ab2[:, pb, :],
                                    start=(I == 0), stop=(I == nI - 1),
                                    waits=w_ready +
                                          ([(sV.h, max(fr_pY, fr_pN))]
                                           if I == 0 else []),
                                    tick=sT.tick())
                            ph.emit("tensor", "matmul", bank(3), ONESb[:, 0:128],
                                    Ab2[:, pb, :],
                                    start=(I == 0), stop=(I == nI - 1),
                                    tick=sT.tick())
                            fr_Ab[pb] = sT.n

                        def v_norm(v, eb=eb, w=sT.n, we=w_er):
                            v.wait_ge(sT.h, w)
                            v.wait_ge(sD.h, we)
                            v.tensor_tensor(brow(3), brow(3), (rowA3 if eb == 0 else rowB3)[0:1, :],
                                            ALU.add)
                            v.reciprocal(rowM3[0:1, :], brow(3)).then_inc(sV.h, 1)
                        ph.raw("vector", v_norm)
                        sV.n += 1
                        fr_eps[eb] = sV.n
                        fr_pN = sV.n
                        ph.emit("tensor", "matmul", brow(4, rows=128), ONESf[0:1, :],
                                rowM3[0:1, :], start=True, stop=True,
                                waits=[(sV.h, sV.n), (sV.h, fr_rnb)],
                                tick=sT.tick())
                        ph.emit("scalar", "activation", Abc3[:, :], bank(4), AF.Copy,
                                waits=[(sT.h, sT.n)], tick=sA.tick())
                        ph.emit("vector", "tensor_tensor", HOy[:, h, :], bank(2),
                                Abc3[:, :], ALU.mult,
                                waits=[(sA.h, sA.n)], tick=sV.tick())
                        fr_pY = sV.n
                        fr_rnb = sV.n
                    w_quad = sV.n

                    fr_wcb = {}
                    for m in range(16):
                        b = m % 2
                        ph.emit("gpsimd", "dma_start", out=ws2[:, b, :, :],
                                in_=wc_p[m, :, :, :],
                                waits=[(sT.h, fr_ws.get(b, 0))], tick=sG.tick(16))

                        def t_wc(te, m=m, b=b, wg=sG.n,
                                 wv_=max(w_quad, w_pre) if m == 0 else 0,
                                 wp=fr_wcb.get(m % 2, 0)):
                            te.wait_ge(sG.h, wg)
                            if wv_:
                                te.wait_ge(sV.h, wv_)
                            if wp:
                                te.wait_ge(sV.h, wp)
                            for k in range(16):
                                inst = te.matmul(bank(6 + m % 2), ws2[:, b, k, :],
                                                 HOy[:, k, :],
                                                 start=(k == 0), stop=(k == 15))
                            inst.then_inc(sT.h, 1)
                        ph.raw("tensor", t_wc)
                        sT.n += 1
                        fr_ws[b] = sT.n
                        ph.emit("vector", "scalar_tensor_tensor",
                                XT[:, m, 512 * tau:512 * tau + 512],
                                bank(6 + m % 2), modcol[:, 32 + m:33 + m],
                                XT[:, m, 512 * tau:512 * tau + 512],
                                ALU.mult, ALU.add,
                                waits=[(sT.h, sT.n)], tick=sV.tick())
                        fr_wcb[m % 2] = sV.n
                    fr_stats3 = sA.n

                if debug:
                    ph.emit("sync", "dma_start", out=dbg["d_xmid"][:, :, :],
                            in_=XT[:, :, :], waits=[(sV.h, sV.n)], tick=sO.tick(16))
                    ph.emit("gpsimd", "dma_start", out=dbg["d_q"][:, :, :],
                            in_=QTt[:, :, :], waits=[(sV.h, sV.n)], tick=sO.tick(16))
                ph.flush()

        with nc.sbuf_tensor("h2", [128, NCH, TO], BF16, side="right") as h2:
            with ExitStack() as p4ctx:
                e4 = p4ctx.enter_context
                xb4 = e4(nc.sbuf_tensor("xb4", [128, 2, 512], BF16, side="right"))
                sq4 = e4(nc.sbuf_tensor("sq4", [128, 2, 512], BF16, side="right"))
                tf4 = e4(nc.sbuf_tensor("tf4", [128, 2, 512], F32, side="right"))
                rowM4 = e4(nc.sbuf_tensor("rowM4", [1, 512], F32, side="right"))
                rowA4 = e4(nc.sbuf_tensor("rowA4", [1, 512], F32, side="right"))
                rowB4 = e4(nc.sbuf_tensor("rowB4", [1, 512], F32, side="right"))
                Abc4 = e4(nc.sbuf_tensor("Abc4", [128, 512], F32, side="right"))
                Bbc4 = e4(nc.sbuf_tensor("Bbc4", [128, 512], F32, side="right"))
                ef = e4(nc.sbuf_tensor("ef", [128, 16], F32, side="right"))
                pf = e4(nc.sbuf_tensor("pf", [128, 16], F32, side="right"))
                qf = e4(nc.sbuf_tensor("qf", [128, 16], F32, side="right"))
                colA = e4(nc.sbuf_tensor("colA", [128, 1], F32, side="right"))
                colB = e4(nc.sbuf_tensor("colB", [128, 1], F32, side="right"))
                colC = e4(nc.sbuf_tensor("colC", [128, 1], F32, side="right"))
                wfull = e4(nc.sbuf_tensor("wfull", [128, 16], F32, side="right"))
                wT = e4(nc.sbuf_tensor("wT", [16, 8, 128], F32, side="right"))

                fr_xb4 = {}
                fr_tf4 = {}
                fr_stats4 = 0
                h2_at = {}
                for tau in range(2):
                    tsl = slice(512 * tau, 512 * tau + 512)
                    for chh in range(NCH):
                        b = chh % 2
                        ph.emit("scalar", "activation", xb4[:, b, :],
                                XT[:, chh, tsl], AF.Copy,
                                waits=([fr_xb4[b]] if b in fr_xb4 else []),
                                tick=sA.tick())
                        ph.emit("scalar", "activation", sq4[:, b, :],
                                XT[:, chh, tsl], AF.Square, tick=sA.tick())
                        ph.emit("tensor", "matmul", bank(4), ONESb[:, 0:128],
                                xb4[:, b, :], start=(chh == 0), stop=(chh == 15),
                                waits=[(sA.h, sA.n)] +
                                      ([(sA.h, fr_stats4)] if chh == 0 else []),
                                tick=sT.tick())
                        ph.emit("tensor", "matmul", bank(5), ONESb[:, 0:128],
                                sq4[:, b, :], start=(chh == 0), stop=(chh == 15),
                                tick=sT.tick())
                        fr_xb4[b] = (sT.h, sT.n)

                    def v_r41(v, w=sT.n):
                        v.wait_ge(sT.h, w)
                        v.tensor_scalar_mul(rowM4[:, :], brow(4), 1.0 / 2048.0)
                        v.tensor_scalar_mul(rowA4[:, :], brow(5), 1.0 / 2048.0)
                        v.tensor_tensor(rowB4[:, :], rowM4[:, :], rowM4[:, :],
                                        ALU.mult)
                        v.tensor_tensor(rowA4[:, :], rowA4[:, :], rowB4[:, :],
                                        ALU.subtract)
                        v.tensor_scalar_add(rowA4[:, :], rowA4[:, :],
                                            1e-5).then_inc(sV.h, 1)
                    ph.raw("vector", v_r41)
                    sV.n += 1
                    ph.emit("scalar", "activation", rowB4[:, :], rowA4[:, :],
                            AF.Sqrt, waits=[(sV.h, sV.n)], tick=sA.tick())

                    def v_r42(v, w=sA.n):
                        v.wait_ge(sA.h, w)
                        v.reciprocal(rowA4[:, :], rowB4[:, :])
                        v.tensor_tensor(rowB4[:, :], rowM4[:, :], rowA4[:, :],
                                        ALU.mult)
                        v.tensor_scalar_mul(rowB4[:, :], rowB4[:, :],
                                            -1.0).then_inc(sV.h, 1)
                    ph.raw("vector", v_r42)
                    sV.n += 1
                    ph.emit("tensor", "matmul", brow(6, rows=128), ONESf[0:1, :],
                            rowA4[0:1, :], start=True, stop=True,
                            waits=[(sV.h, sV.n)], tick=sT.tick())
                    ph.emit("tensor", "matmul", brow(7, rows=128), ONESf[0:1, :],
                            rowB4[0:1, :], start=True, stop=True, tick=sT.tick())
                    ph.emit("scalar", "activation", Abc4[:, :], bank(6), AF.Copy,
                            waits=[(sT.h, sT.n)], tick=sA.tick())
                    ph.emit("scalar", "activation", Bbc4[:, :], bank(7), AF.Copy,
                            tick=sA.tick())
                    w_abc = sA.n

                    for chh in range(NCH):
                        b = chh % 2
                        ph.emit("scalar", "activation", xb4[:, b, :],
                                XT[:, chh, tsl], AF.Copy,
                                waits=[(sA.h, w_abc)] +
                                      ([fr_xb4[b]] if b in fr_xb4 else []),
                                tick=sA.tick())

                        def v_h4(v, b=b, w=sA.n, wtf=fr_tf4.get(b, 0)):
                            v.wait_ge(sA.h, max(w, wtf))
                            v.tensor_tensor(tf4[:, b, :], xb4[:, b, :], Abc4[:, :],
                                            ALU.mult)
                            v.tensor_tensor(tf4[:, b, :], tf4[:, b, :],
                                            Bbc4[:, :], ALU.add).then_inc(sV.h, 1)
                        ph.raw("vector", v_h4)
                        sV.n += 1
                        fr_xb4[b] = (sV.h, sV.n)
                        ph.emit("scalar", "activation", h2[:, chh, tsl],
                                tf4[:, b, :], AF.Identity,
                                scale=modcol[:, 64 + chh:65 + chh],
                                bias=modcol[:, 48 + chh:49 + chh],
                                waits=[(sV.h, sV.n)], tick=sA.tick())
                        fr_tf4[b] = sA.n
                        h2_at[(tau, chh)] = sA.n
                    fr_stats4 = sA.n
                w_h2 = sA.n

                fr_pL = {}
                fr_tr = 0
                for t in range(8):
                    pb = t % 2

                    def t_log(te, t=t, pb=pb, wp=fr_pL.get(pb, 0),
                              h2a=dict(h2_at)):
                        if wp:
                            te.wait_ge(sV.h, wp)
                        for chh in range(NCH):
                            te.wait_ge(sA.h, h2a[(t // 4, chh)])
                            te.matmul(brow(4 + pb, rows=128, width=16),
                                      h2[:, chh, 128 * t:128 * (t + 1)],
                                      wr_sb[:, chh, :],
                                      start=(chh == 0), stop=False)
                        te.matmul(brow(4 + pb, rows=128, width=16),
                                  ONESb[0:1, 0:128], fb_bf[0:1, :],
                                  start=False, stop=True).then_inc(sT.h, 1)
                    ph.raw("tensor", t_log)
                    sT.n += 1

                    def v_sm(v, pb=pb, w=sT.n):
                        v.wait_ge(sT.h, w)
                        pL = brow(4 + pb, rows=128, width=16)
                        v.reduce_max(colA[:, :], pL, AX.X,
                                     negate=True).then_inc(sV.h, 1)
                    ph.raw("vector", v_sm)
                    sV.n += 1
                    ph.emit("scalar", "activation", ef[:, :],
                            brow(4 + pb, rows=128, width=16), AF.Exp,
                            bias=colA[:, :], scale=1.0,
                            waits=[(sV.h, sV.n)], tick=sA.tick())

                    def v_sm2(v, w=sA.n):
                        v.wait_ge(sA.h, w)
                        v.reduce_sum(colB[:, :], ef[:, :], AX.X)
                        v.reciprocal(colB[:, :], colB[:, :])
                        v.tensor_scalar_mul(pf[:, :], ef[:, :], colB[:, :])
                        v.reduce_max(colA[:, :], pf[:, :], AX.X)
                        v.tensor_single_scalar(qf[:, :], pf[:, :], colA[:, :],
                                               ALU.is_ge)
                        v.scalar_tensor_tensor(qf[:, :], qf[:, :], -1e30,
                                               pf[:, :], ALU.mult, ALU.add)
                        v.reduce_max(colB[:, :], qf[:, :], AX.X)
                        v.tensor_tensor(colC[:, :], colA[:, :], colB[:, :], ALU.add)
                        v.tensor_scalar_add(colC[:, :], colC[:, :], 1e-9)
                        v.reciprocal(colC[:, :], colC[:, :])
                        v.tensor_single_scalar(qf[:, :], pf[:, :], colB[:, :],
                                               ALU.is_ge)
                        v.tensor_tensor(wfull[:, :], pf[:, :], qf[:, :], ALU.mult)
                        v.tensor_scalar_mul(wfull[:, :], wfull[:, :],
                                            colC[:, :]).then_inc(sV.h, 1)
                    ph.raw("vector", v_sm2)
                    sV.n += 1
                    fr_pL[pb] = sV.n

                    ph.emit("tensor", "transpose",
                            PS[0:16, 3072 + 128 * (t % 4):3072 + 128 * (t % 4) + 128],
                            wfull[:, :], idf[:, :],
                            waits=[(sV.h, sV.n), (sA.h, fr_tr)], tick=sT.tick())
                    ph.emit("scalar", "activation", wT[:, t, :],
                            PS[0:16, 3072 + 128 * (t % 4):3072 + 128 * (t % 4) + 128],
                            AF.Copy, waits=[(sT.h, sT.n)], tick=sA.tick())
                    if t % 4 == 3:
                        fr_tr = sA.n

                ph.emit("sync", "dma_start", out=wscr[:, :], in_=wT[:, :, :],
                        waits=[(sA.h, sA.n)], tick=sD.tick(16))
                w_wscr_out = sD.n

                if debug:
                    ph.emit("gpsimd", "dma_start", out=dbg["d_h2"][:, :, :],
                            in_=h2[:, :, :], waits=[(sA.h, sA.n)], tick=sO.tick(16))
                    ph.emit("sync", "dma_start", out=dbg["d_w"][:, :, :],
                            in_=wT[:, :, :], tick=sO.tick(16))

            with ExitStack() as p5ctx:
                e5 = p5ctx.enter_context
                AE = e5(nc.sbuf_tensor("AE", [128, 8, TO], BF16, side="right"))
                wgs = e5(nc.sbuf_tensor("wgs", [128, 2, 16, 128], BF16, side="right"))
                wus = e5(nc.sbuf_tensor("wus", [128, 2, 16, 128], BF16, side="right"))
                wds = e5(nc.sbuf_tensor("wds", [128, 2, 8, 128], BF16, side="right"))
                wrow = e5(nc.sbuf_tensor("wrow", [1, 2, TO], F32, side="right"))
                wbc = e5(nc.sbuf_tensor("wbc", [128, 2, 512], F32, side="right"))
                sgb = e5(nc.sbuf_tensor("sgb", [128, 2, 512], F32, side="right"))
                usb = e5(nc.sbuf_tensor("usb", [128, 2, 512], F32, side="right"))

                fr_wr = {}
                fr_wg = {}
                fr_wd = {}
                fr_gu = {}
                fr_sg = {}
                fr_us = {}
                fr_ae_rd = 0
                fr_wbc_rd = 0
                fr_wb = 0
                fr_pr = {}
                for e in range(NE):
                    shared = (e == NE - 1)
                    eb = e % 2
                    if not shared:
                        ph.emit("sync", "dma_start", out=wrow[0:1, eb, :],
                                in_=wscr[e:e + 1, :],
                                waits=[(sD.h, w_wscr_out),
                                       (sA.h, fr_wr.get(eb, 0))],
                                tick=sD.tick(16))
                        w_wr = sD.n

                        def t_wbc(te, eb=eb, w=w_wr, wb=fr_wb):
                            te.wait_ge(sD.h, w)
                            if wb:
                                te.wait_ge(sA.h, wb)
                            for tau2 in range(2):
                                te.matmul(brow(6 + tau2, rows=128), ONESf[0:1, :],
                                          wrow[0:1, eb, 512 * tau2:512 * (tau2 + 1)],
                                          start=True,
                                          stop=True).then_inc(sT.h, 1)
                        ph.raw("tensor", t_wbc)
                        sT.n += 2

                        def a_wbc(sc, w=sT.n, wrd=fr_wbc_rd):
                            sc.wait_ge(sT.h, w)
                            if wrd:
                                sc.wait_ge(sV.h, wrd)
                            sc.activation(wbc[:, 0, :], bank(6),
                                          AF.Copy).then_inc(sA.h, 1)
                            sc.activation(wbc[:, 1, :], bank(7),
                                          AF.Copy).then_inc(sA.h, 1)
                        ph.raw("scalar", a_wbc)
                        sA.n += 2
                        fr_wr[eb] = sA.n
                        fr_wb = sA.n
                        w_wbce = sA.n
                    else:
                        w_wbce = 0

                    for i in range(8):
                        gb_ = i % 2
                        ph.emit("gpsimd", "dma_start", out=wgs[:, gb_, :, :],
                                in_=wg_p[e, i, :, :, :],
                                waits=[(sT.h, fr_wg.get(gb_, 0))] +
                                      ([(sD.h, w_wscr_out)]
                                       if (e == 0 and i == 0) else []),
                                tick=sG.tick(16))
                        ph.emit("gpsimd", "dma_start", out=wus[:, gb_, :, :],
                                in_=wu_p[e, i, :, :, :], tick=sG.tick(16))
                        w_wgu = sG.n

                        for tau2 in range(2):
                            pb = (2 * i + tau2) % 2

                            def t_gu(te, tau2=tau2, gb_=gb_, pb=pb, w=w_wgu,
                                     wpf=fr_gu.get(pb),
                                     h2a=(h2_at[(tau2, 15)]
                                          if (e == 0 and i == 0) else 0)):
                                te.wait_ge(sG.h, w)
                                if wpf:
                                    te.wait_ge(sA.h, wpf[0])
                                    te.wait_ge(sV.h, wpf[1])
                                if h2a:
                                    te.wait_ge(sA.h, h2a)
                                for chh in range(NCH):
                                    te.matmul(bank(pb), wgs[:, gb_, chh, :],
                                              h2[:, chh,
                                                 512 * tau2:512 * (tau2 + 1)],
                                              start=(chh == 0), stop=(chh == 15))
                                for chh in range(NCH):
                                    inst = te.matmul(
                                        bank(2 + pb), wus[:, gb_, chh, :],
                                        h2[:, chh, 512 * tau2:512 * (tau2 + 1)],
                                        start=(chh == 0), stop=(chh == 15))
                                inst.then_inc(sT.h, 1)
                            ph.raw("tensor", t_gu)
                            sT.n += 1
                            fr_wg[gb_] = sT.n

                            ph.emit("scalar", "activation", sgb[:, pb, :], bank(pb),
                                    AF.Silu,
                                    waits=[(sT.h, sT.n),
                                           (sV.h, fr_sg.get(pb, 0))],
                                    tick=sA.tick())
                            w_sg = sA.n

                            def v_ae(v, i=i, tau2=tau2, pb=pb, shared=shared,
                                     w=w_sg, wb=w_wbce, wrd=fr_ae_rd):
                                v.wait_ge(sA.h, max(w, wb))
                                if wrd:
                                    v.wait_ge(sT.h, wrd)
                                if shared:
                                    v.tensor_tensor(
                                        AE[:, i, 512 * tau2:512 * (tau2 + 1)],
                                        bank(2 + pb), sgb[:, pb, :],
                                        ALU.mult).then_inc(sV.h, 1)
                                else:
                                    v.tensor_tensor(usb[:, pb, :], bank(2 + pb),
                                                    wbc[:, tau2, :], ALU.mult)
                                    v.tensor_tensor(
                                        AE[:, i, 512 * tau2:512 * (tau2 + 1)],
                                        usb[:, pb, :], sgb[:, pb, :],
                                        ALU.mult).then_inc(sV.h, 1)
                            ph.raw("vector", v_ae)
                            sV.n += 1
                            fr_gu[pb] = (sA.n, sV.n)
                            fr_sg[pb] = sV.n
                    w_ae = sV.n
                    fr_wbc_rd = sV.n

                    for c in range(16):
                        db_ = c % 2
                        ph.emit("gpsimd", "dma_start", out=wds[:, db_, :, :],
                                in_=wd_p[e, c, :, :, :],
                                waits=[(sT.h, fr_wd.get(db_, 0))], tick=sG.tick(16))
                        w_wd = sG.n

                        for tau2 in range(2):
                            pb = (2 * c + tau2) % 2

                            def t_wd(te, tau2=tau2, db_=db_, pb=pb, w=w_wd,
                                     wae=w_ae if (c == 0 and tau2 == 0) else 0,
                                     wp=fr_pr.get(pb, 0)):
                                te.wait_ge(sG.h, w)
                                if wae:
                                    te.wait_ge(sV.h, wae)
                                if wp:
                                    te.wait_ge(sV.h, wp)
                                for k in range(8):
                                    inst = te.matmul(
                                        bank(4 + pb), wds[:, db_, k, :],
                                        AE[:, k, 512 * tau2:512 * (tau2 + 1)],
                                        start=(k == 0), stop=(k == 7))
                                inst.then_inc(sT.h, 1)
                            ph.raw("tensor", t_wd)
                            sT.n += 1
                            fr_wd[db_] = sT.n
                            fr_ae_rd = sT.n

                            ph.emit("vector", "scalar_tensor_tensor",
                                    XT[:, c, 512 * tau2:512 * (tau2 + 1)],
                                    bank(4 + pb), modcol[:, 80 + c:81 + c],
                                    XT[:, c, 512 * tau2:512 * (tau2 + 1)],
                                    ALU.mult, ALU.add,
                                    waits=[(sT.h, sT.n)], tick=sV.tick())
                            fr_pr[pb] = sV.n
                w_ffn = sV.n

                def s_out(s_, w=w_ffn):
                    s_.wait_ge(sV.h, w)
                    s_.dma_start(out=yT_d.rearrange("(c p) t -> p c t", p=128),
                                 in_=XT[:, :, :]).then_inc(sO.h, 16)
                ph.raw("sync", s_out)
                sO.n += 16
                fin = sO.n

                def s_fin(s_, w=fin):
                    s_.wait_ge(sO.h, w)
                ph.raw("sync", s_fin)
                ph.flush()

    return nc


def _host_layout(inputs):
    x = np.asarray(inputs["L0_attn_hidden_states"], dtype=np.float32)
    Wq = np.asarray(inputs["L0_attn_Wq"], dtype=np.float32)
    Wk = np.asarray(inputs["L0_attn_Wk"], dtype=np.float32)
    Wv = np.asarray(inputs["L0_attn_Wv"], dtype=np.float32)
    Wg = np.asarray(inputs["L0_attn_Wg"], dtype=np.float32)
    Wc = np.asarray(inputs["L0_attn_Wc"], dtype=np.float32)
    bc = np.asarray(inputs["L0_attn_bc"], dtype=np.float32)
    Wr = np.asarray(inputs["L0_ffn_Wr"], dtype=np.float32)
    fbias = np.asarray(inputs["L0_ffn_bias"], dtype=np.float32)
    Wge = np.asarray(inputs["L0_ffn_Wg"], dtype=np.float32)
    Wue = np.asarray(inputs["L0_ffn_Wu"], dtype=np.float32)
    Wde = np.asarray(inputs["L0_ffn_Wd"], dtype=np.float32)
    sWg = np.asarray(inputs["L0_ffn_sWg"], dtype=np.float32)
    sWu = np.asarray(inputs["L0_ffn_sWu"], dtype=np.float32)
    sWd = np.asarray(inputs["L0_ffn_sWd"], dtype=np.float32)
    cmat = np.asarray(inputs["L0_c"], dtype=np.float32)
    wmod = np.asarray(inputs["L0_wmod"], dtype=np.float32)

    def perm_lhs(W):
        ci, co = W.shape
        return np.ascontiguousarray(
            W.reshape(ci // 128, 128, co // 128, 128).transpose(2, 1, 0, 3))

    wq_pm = perm_lhs(Wq)
    wk_pm = perm_lhs(Wk)
    wc_pm = perm_lhs(Wc)
    wgat_pm = np.ascontiguousarray(Wg.reshape(16, 128, 8).transpose(1, 0, 2))
    wr_pm = np.ascontiguousarray(
        np.ascontiguousarray(Wr.T).reshape(16, 128, 16).transpose(1, 0, 2))
    bc_pm = np.ascontiguousarray(bc.reshape(16, 128).T)

    wg_all = np.concatenate([Wge, sWg[None]], axis=0)
    wu_all = np.concatenate([Wue, sWu[None]], axis=0)
    wd_all = np.concatenate([Wde, sWd[None]], axis=0)
    wg_pm = np.ascontiguousarray(
        wg_all.reshape(NE, 16, 128, 8, 128).transpose(0, 3, 2, 1, 4))
    wu_pm = np.ascontiguousarray(
        wu_all.reshape(NE, 16, 128, 8, 128).transpose(0, 3, 2, 1, 4))
    wd_pm = np.ascontiguousarray(
        wd_all.reshape(NE, 8, 128, 16, 128).transpose(0, 3, 2, 1, 4))

    shared = dict(wq_p=wq_pm, wk_p=wk_pm, wv_n=Wv, wc_p=wc_pm, wgat_p=wgat_pm,
                  wmod_n=wmod, wr_p=wr_pm, fbias=fbias.reshape(1, 16),
                  bc_p=bc_pm, wg_p=wg_pm, wu_p=wu_pm, wd_p=wd_pm)

    in_maps = []
    own_tok = {}
    for core in range(8):
        b = core // 2
        j = core % 2
        blocks = OWN_BLOCKS[j]
        toks = np.concatenate([np.arange(128 * bl, 128 * (bl + 1))
                               for bl in blocks])
        own_tok[core] = toks
        xb = x[b]
        xT = np.ascontiguousarray(xb.T)
        xTo = np.ascontiguousarray(xb.T[:, toks])
        csT = np.ascontiguousarray(cmat[b].reshape(16, 128).T)

        thrm = np.zeros((128, 32), dtype=np.float32)
        for tau in range(2):
            T0 = TILE_T0[j][tau]
            for I in range(16):
                if I in TILE_MASKED[tau]:
                    if 128 * I >= T0 + 512:
                        val = -1e9
                    else:
                        val = float(T0 - 128 * I)
                else:
                    val = 1e9
                thrm[:, 16 * tau + I] = val

        mrep = np.zeros((2, 8, T), dtype=np.float32)
        for r in range(2):
            start = TILE_T0[j][r]
            mrep[r, :, :start] = 1.0

        m = dict(shared)
        m.update(xT_full=xT, xT_own=xTo, csT=csT, thr=thrm, maskrep=mrep)
        in_maps.append(m)
    return in_maps, own_tok


_PROG_CACHE = {}


def kernel(**inputs):
    debug = bool(inputs.pop("_debug", False))
    in_maps, own_tok = _host_layout(inputs)
    key = ("prog", debug)
    if key not in _PROG_CACHE:
        _PROG_CACHE[key] = build_program(debug=debug)
    nc = _PROG_CACHE[key]
    res = run_bass_kernel_spmd(nc, in_maps, core_ids=list(range(8)))
    out = np.zeros((4, T, C), dtype=np.float32)
    dbg_out = []
    for core in range(8):
        b = core // 2
        r = res.results[core]
        out[b, own_tok[core], :] = np.asarray(r["yT"]).T
        if debug:
            dbg_out.append(r)
    if debug:
        kernel._dbg = dbg_out
    return out
